# version 31
# baseline (speedup 1.0000x reference)
import numpy as np
from contextlib import ExitStack

import concourse.bass as bass
import concourse.mybir as mybir
from concourse.bass_utils import run_bass_kernel_spmd

F32 = mybir.dt.float32
BF16 = mybir.dt.bfloat16
U8 = mybir.dt.uint8
AF = mybir.ActivationFunctionType
ALU = mybir.AluOpType
AXX = mybir.AxisListType.X

NCORES = 8
L = 16384
D = 1024
CT = 8
T = 2048
WARM = 1536
SLAB = 128 + WARM + T
TAU = 512
NTAU = (WARM + T) // TAU
NWT = WARM // TAU
SCT = 12
FFP = 2816
FFT = FFP // 128
MH = 4096
NEXP = 8
PASS_T = 1024
NPASS = T // PASS_T

LN32 = float(np.log(32.0))
ENGS = ("sync", "gpsimd", "tensor", "scalar", "vector")

TAPS = False
import os
SKIP_MOE = bool(os.environ.get("SKIP_MOE"))
SKIP_SPHASE = bool(os.environ.get("SKIP_SPHASE"))
JUNK_GROUP = not bool(os.environ.get("NO_JUNK_GROUP"))


class Sem:
    __slots__ = ("h", "v")

    def __init__(self, h):
        self.h = h
        self.v = 0


class Sched:
    def __init__(self, nc, stack):
        self.nc = nc
        self.stack = stack
        self.ops = {k: [] for k in ENGS}

    def sem(self, name):
        return Sem(self.stack.enter_context(self.nc.semaphore(name)))

    def emit(self, eng, fn, waits=(), incs=(), raw_incs=()):
        self.ops[eng].append((fn, tuple(waits), tuple(incs) + tuple(raw_incs)))
        for s, k in incs:
            s.v += k

    def set_pending(self, sem):
        assert getattr(self, "pending", None) is None
        sem.v += 1
        self.pending = sem

    def pop_pending(self):
        p = getattr(self, "pending", None)
        self.pending = None
        return p

    def flush_pending(self):
        p = self.pop_pending()
        if p is not None:
            self.tmark_raw(p)

    def flush(self):
        ops = self.ops
        self.ops = {k: [] for k in ENGS}

        def mk(eng):
            seq = ops[eng]

            def f(e):
                lastw = {}
                for fn, waits, incs in seq:
                    for (s, v) in waits:
                        if v > 0 and lastw.get(id(s), -1) < v:
                            e.wait_ge(s.h, v)
                            lastw[id(s)] = v
                    instr = fn(e)
                    if incs:
                        assert instr is not None, f"{eng} op returned None"
                        for s, k in incs:
                            instr = instr.then_inc(s.h, k)
            return f

        with self.nc.Block() as blk:
            for eng in ENGS:
                if ops[eng]:
                    getattr(blk, eng)(mk(eng))


class PSlot:
    def __init__(self, sched, ap, name):
        self.ap = ap
        self.prod = sched.sem(name + "_p")
        self.free = sched.sem(name + "_f")
        self.reads = 0


def mm_group(sch, slot, out_ap, mats, readers, extra_waits=(),
             defer_prod=False):
    prev = slot.reads
    n = len(mats)
    pend = sch.pop_pending()

    def tf(e, mats=mats, out_ap=out_ap, n=n, pend=pend):
        last = None
        lastw = {}
        for i, mat in enumerate(mats):
            if len(mat) == 3:
                for (s, v) in mat[2]:
                    if v > 0 and lastw.get(id(s), -1) < v:
                        e.wait_ge(s.h, v)
                        lastw[id(s)] = v
            last = e.matmul(out_ap, mat[0], mat[1],
                            start=(i == 0), stop=(i == n - 1))
            if i == 0 and pend is not None:
                last.then_inc(pend.h, 1)
        return last

    sch.emit("tensor", tf, waits=tuple(extra_waits) + ((slot.free, prev),))
    if defer_prod:
        sch.set_pending(slot.prod)
    else:
        sch.tmark(((slot.prod, 1),))
    pv = slot.prod.v
    for rd in readers:
        eng, fn = rd[0], rd[1]
        ew = rd[2] if len(rd) >= 3 else ()
        sch.emit(eng, fn, waits=((slot.prod, pv),) + tuple(ew),
                 incs=((slot.free, 1),))
        slot.reads += 1


def build_program(taps=False):
    nc = bass.Bass("TRN2", target_bir_lowering=False)

    xs_d = nc.dram_tensor("xs", [SLAB, D], F32, kind="ExternalInput").ap()
    pwT_d = nc.dram_tensor("pwT", [D, D], F32, kind="ExternalInput").ap()
    whg_d = nc.dram_tensor("whg", [D, 768], F32, kind="ExternalInput").ap()
    wout_d = nc.dram_tensor("wout", [1536, 256], F32, kind="ExternalInput").ap()
    dwd_d = nc.dram_tensor("dwdiag", [128, 4096], F32,
                           kind="ExternalInput").ap()
    dwb_d = nc.dram_tensor("dwb", [128, 8], F32, kind="ExternalInput").ap()
    pwb_d = nc.dram_tensor("pwb", [128, 8], F32, kind="ExternalInput").ap()
    gc_d = nc.dram_tensor("gc", [128, 8], F32, kind="ExternalInput").ap()
    gg_d = nc.dram_tensor("gg", [128, 8], F32, kind="ExternalInput").ap()
    gf_d = nc.dram_tensor("gf", [128, 8], F32, kind="ExternalInput").ap()
    wgate_d = nc.dram_tensor("wgate", [D, FFP], F32, kind="ExternalInput").ap()
    wval_d = nc.dram_tensor("wval", [D, FFP], F32, kind="ExternalInput").ap()
    wffo_d = nc.dram_tensor("wffo", [FFP, D], F32, kind="ExternalInput").ap()
    moewg_d = nc.dram_tensor("moewg", [128, 64], F32, kind="ExternalInput").ap()
    bg_d = nc.dram_tensor("bgx", [1, 8], F32, kind="ExternalInput").ap()
    b1c_d = nc.dram_tensor("b1c", [128, 256], F32, kind="ExternalInput").ap()
    b2r_d = nc.dram_tensor("b2r", [1, 8192], F32, kind="ExternalInput").ap()
    w1_d = nc.dram_tensor("w1", [NEXP, D, MH], F32, kind="ExternalInput").ap()
    w2_d = nc.dram_tensor("w2", [NEXP, MH, D], F32, kind="ExternalInput").ap()
    id_d = nc.dram_tensor("ident", [128, 128], F32, kind="ExternalInput").ap()
    zsc_d = nc.dram_tensor("zsc", [128, 1], F32, kind="ExternalInput").ap()
    out_d = nc.dram_tensor("out", [T, D], F32, kind="ExternalOutput").ap()

    s_dram = nc.dram_tensor("s_sp", [T, D], F32).ap()
    wgate_b = nc.dram_tensor("wgate_b", [D, FFP], BF16).ap()
    wval_b = nc.dram_tensor("wval_b", [D, FFP], BF16).ap()
    wffo_b = nc.dram_tensor("wffo_b", [FFP, D], BF16).ap()
    if taps:
        xn_dram = nc.dram_tensor("xn_sp", [CT, 128, T], BF16,
                                 kind="ExternalOutput").ap()
    else:
        xn_dram = nc.dram_tensor("xn_sp", [CT, 128, T], BF16).ap()
    tap_aps = {}
    if taps:
        tap_aps["tap_u"] = nc.dram_tensor("tap_u", [CT, 128, T], F32,
                                          kind="ExternalOutput").ap()
        tap_aps["tap_h"] = nc.dram_tensor("tap_h", [SCT, 128, T], F32,
                                          kind="ExternalOutput").ap()
        tap_aps["tap_s"] = nc.dram_tensor("tap_s", [T, D], F32,
                                          kind="ExternalOutput").ap()
        tap_aps["tap_g"] = nc.dram_tensor("tap_g", [128, NPASS * 8192], F32,
                                          kind="ExternalOutput").ap()
        tap_aps["tap_w"] = nc.dram_tensor("tap_w", [128, NPASS * 64], F32,
                                          kind="ExternalOutput").ap()
        tap_aps["tap_n2"] = nc.dram_tensor("tap_n2", [128, CT * 512], F32,
                                           kind="ExternalOutput").ap()
        tap_aps["tap_ff1"] = nc.dram_tensor("tap_ff1", [128, FFT * 512], F32,
                                            kind="ExternalOutput").ap()
        tap_aps["tap_h1"] = nc.dram_tensor("tap_h1", [128, 16 * PASS_T], F32,
                                           kind="ExternalOutput").ap()
        tap_aps["tap_y"] = nc.dram_tensor("tap_y", [128, NPASS * 8192], F32,
                                          kind="ExternalOutput").ap()

    with ExitStack() as big:
        sch = Sched(nc, big)
        sb = lambda name, shape, dt=F32: big.enter_context(
            nc.sbuf_tensor(name, shape, dt))

        ident = sb("identsb", [128, 128])
        onesf = sb("onesf", [1, 128])
        onesk = sb("onesk", [128, 1], BF16)
        ones_bf = sb("ones_bf", [1, 128], BF16)
        eps_ln = sb("eps_ln", [128, 1])
        eps_rms = sb("eps_rms", [1, 1])
        ln32t = sb("ln32t", [1, 1])
        zsc = sb("zsc_sb", [128, 1])

        pjunk = None if os.environ.get("NO_PJUNK") else big.enter_context(
            nc.psum_tensor("pjunk", [128, 512], F32))
        P = [PSlot(sch, big.enter_context(
            nc.psum_tensor(f"pmm{i}", [128, 512], F32)), f"pmm{i}")
            for i in range(4)]

        def tmark(incs, waits=()):
            pend = sch.pop_pending()
            if pend is not None:
                sch.tmark_raw(pend)

            def f(e):
                return e.matmul(pjunk[0:1, 0:1], ones_bf[0:1, 0:1],
                                ones_bf[0:1, 0:1], start=True, stop=True)
            sch.emit("tensor", f, waits=waits, incs=incs)

        def tmark_raw(sem):
            def f(e, sem=sem):
                i = e.matmul(pjunk[0:1, 0:1], ones_bf[0:1, 0:1],
                             ones_bf[0:1, 0:1], start=True, stop=True)
                return i.then_inc(sem.h, 1)
            sch.emit("tensor", f)
        sch.tmark = tmark
        sch.tmark_raw = tmark_raw
        sch.pending = None
        _pi = [0]

        def pnext():
            s = P[_pi[0] % 4]
            _pi[0] += 1
            return s

        ld = sch.sem("ld")

        def g_dma(out_ap, in_ap, waits=()):
            def f(e, o=out_ap, i=in_ap):
                return e.dma_start(out=o, in_=i)
            sch.emit("gpsimd", f, waits=waits, incs=((ld, 16),))
            return ld.v

        def g_memset(ap, val):
            def f(e, a=ap, v=val):
                return e.memset(a, v)
            sch.emit("gpsimd", f)

        tapq = sch.sem("tapq")

        def s_dma(out_ap, in_ap, waits=(), inc=None):
            if inc is None:
                inc = tapq
            def f(e, o=out_ap, i=in_ap):
                return e.dma_start(out=o, in_=i)
            sch.emit("sync", f, waits=waits, incs=((inc, 16),))

        with ExitStack() as mix:
            msb = lambda name, shape, dt=F32: mix.enter_context(
                nc.sbuf_tensor(name, shape, dt))

            pwT = msb("pwT_bf", [128, CT * D], BF16)
            whg = msb("whg_bf", [128, CT * 768], BF16)
            wout = msb("wout_bf", [128, SCT * 256], BF16)
            dwd_bf = msb("dwd_bf", [128, 4096], BF16)
            dwb = msb("dwb_sb", [128, 8])
            pwb = msb("pwb_sb", [128, 8])
            dwb_bf = msb("dwb_bf", [128, 8], BF16)
            gc1 = msb("gc1", [128, 8])
            gg1 = msb("gg1", [128, 8])
            gf1 = msb("gf1", [128, 8])
            bcomb = msb("bcomb", [128, 8])
            carry = msb("carry", [128, SCT])

            x_tm = [msb(f"x_tm{i}", [128, D]) for i in range(2)]
            lnt = [msb(f"lnt{i}", [128, 8]) for i in range(2)]
            lnsq = msb("lnsq", [128, 1024])
            act32 = msb("act32", [128, CT * 512])
            scrq = msb("scrq", [128, 2 * 512], BF16)
            c_ext = msb("c_ext", [128, CT * 515], BF16)
            bfA = msb("bfA", [128, CT * 512], BF16)
            bfN = msb("bfN", [128, CT * 512], BF16)
            rrow = msb("rrow", [1, 512])
            h32 = msb("h32", [128, 2 * 512])
            h_bf = msb("h_bf", [128, SCT * 512], BF16)
            a32 = msb("a32", [128, 2 * 512])
            z32 = msb("z32", [128, 2 * 512])
            sg32 = msb("sg32", [128, 2 * 512])
            gh32 = msb("gh32", [128, 2 * 512])
            ff1 = msb("ff1_bf", [128, FFT * 512], BF16)
            sil = msb("sil", [128, 2 * 512])
            v_tm = msb("v_tm", [128, 4 * D])
            s_sp = [msb(f"s_sp{i}", [128, 512]) for i in range(2)]
            xn_stg = [msb(f"xn_stg{i}", [128, 512], BF16) for i in range(2)]
            wgs = msb("wgs", [128, 2 * CT * 128], BF16)
            wvs = msb("wvs", [128, 2 * CT * 128], BF16)
            wos = msb("wos", [128, 4 * D], BF16)

            P_tr = PSlot(sch, mix.enter_context(
                nc.psum_tensor("ptr", [128, 512], F32)), "ptr")
            P_row = PSlot(sch, mix.enter_context(
                nc.psum_tensor("prow", [1, 512], F32)), "prow")
            P_bc = PSlot(sch, mix.enter_context(
                nc.psum_tensor("pbc", [128, 512], F32)), "pbc")

            g_dma(pwT[:].rearrange("p (k e) -> p k e", e=D),
                  pwT_d.rearrange("(k p) e -> p k e", p=128))
            g_dma(whg[:].rearrange("p (k e) -> p k e", e=768),
                  whg_d.rearrange("(k p) e -> p k e", p=128))
            g_dma(wout[:].rearrange("p (k e) -> p k e", e=256),
                  wout_d.rearrange("(k p) e -> p k e", p=128))
            g_dma(dwd_bf[:], dwd_d)
            g_dma(dwb[:], dwb_d)
            g_dma(pwb[:], pwb_d)
            g_dma(gc1[:], gc_d)
            g_dma(gg1[:], gg_d)
            g_dma(gf1[:], gf_d)
            g_dma(ident[:], id_d)
            g_dma(zsc[:], zsc_d)
            ldtot = ld.v
            g_memset(onesf[:], 1.0)
            g_memset(onesk[:], 1.0)
            g_memset(ones_bf[:], 1.0)
            g_memset(eps_ln[:], 1e-5)
            g_memset(eps_rms[:], 1e-18)
            g_memset(ln32t[:], LN32)
            g_memset(carry[:], 0.0)
            init_gp = sch.sem("init_gp")
            sch.emit("gpsimd", lambda e: e.engine_nop(),
                     waits=((ld, ldtot),), incs=((init_gp, 1),))

            gam_done = sch.sem("gam_done")
            for t_ in (gc1, gg1, gf1):
                def f(e, t_=t_):
                    return e.add(t_[:], t_[:], 1.0)
                sch.emit("scalar", f, waits=((init_gp, 1),),
                         incs=((gam_done, 1),))

            vcd = sch.sem("vcd")
            sch.emit("vector", lambda e: e.tensor_copy(dwb_bf[:], dwb[:]),
                     waits=((init_gp, 1),), incs=((vcd, 1),))

            for m in range(CT):
                slot = pnext()
                mats = [(pwT[:, k * D + m * 128:k * D + (m + 1) * 128],
                         dwb_bf[:, k:k + 1]) for k in range(CT)]

                def rd(e, m=m, s=slot):
                    return e.scalar_tensor_tensor(
                        bcomb[:, m:m + 1], s.ap[:, 0:1], 1.0,
                        pwb[:, m:m + 1], op0=ALU.mult, op1=ALU.add)
                mm_group(sch, slot, slot.ap[:, 0:1], mats, [("vector", rd)],
                         extra_waits=((vcd, 1), (init_gp, 1)))

            sch.flush()

            gate_ov, val_ov, wo_ov = {}, {}, {}
            for c in range(FFT):
                prs = c % 2
                w_in = ((ld, gate_ov[c - 2]),) if c >= 2 else ()
                iv = g_dma(wgs[:, prs * CT * 128:(prs + 1) * CT * 128]
                           .rearrange("p (k m) -> p k m", m=128),
                           bass.AP(wgate_d.tensor, c * 128,
                                   [[FFP, 128], [128 * FFP, CT], [1, 128]]),
                           waits=w_in)
                gate_ov[c] = g_dma(
                    bass.AP(wgate_b.tensor, c * 128,
                            [[FFP, 128], [128 * FFP, CT], [1, 128]]),
                    wgs[:, prs * CT * 128:(prs + 1) * CT * 128]
                    .rearrange("p (k m) -> p k m", m=128),
                    waits=((ld, iv),))
                w_in = ((ld, val_ov[c - 2]),) if c >= 2 else ()
                iv = g_dma(wvs[:, prs * CT * 128:(prs + 1) * CT * 128]
                           .rearrange("p (k m) -> p k m", m=128),
                           bass.AP(wval_d.tensor, c * 128,
                                   [[FFP, 128], [128 * FFP, CT], [1, 128]]),
                           waits=w_in)
                val_ov[c] = g_dma(
                    bass.AP(wval_b.tensor, c * 128,
                            [[FFP, 128], [128 * FFP, CT], [1, 128]]),
                    wvs[:, prs * CT * 128:(prs + 1) * CT * 128]
                    .rearrange("p (k m) -> p k m", m=128),
                    waits=((ld, iv),))
            for k in range(FFT):
                prs = k % 4
                w_in = ((ld, wo_ov[k - 4]),) if k >= 4 else ()
                iv = g_dma(wos[:, prs * D:(prs + 1) * D],
                           wffo_d[k * 128:(k + 1) * 128, :], waits=w_in)
                wo_ov[k] = g_dma(wffo_b[k * 128:(k + 1) * 128, :],
                                 wos[:, prs * D:(prs + 1) * D],
                                 waits=((ld, iv),))
            pc_all = wo_ov[FFT - 1]

            xq = [sch.sem("xq0"), sch.sem("xq1")]
            xfree = sch.sem("xfree")
            ln_vs = sch.sem("ln_vs")
            ln_sv = sch.sem("ln_sv")
            xn_rdy = sch.sem("xn_rdy")
            sq_done = sch.sem("sq_done")
            stats_mm = sch.sem("stats_mm")
            rr_done = sch.sem("rr_done")
            ap_done = sch.sem("ap_done")
            y1_done = sch.sem("y1_done")
            u_done = sch.sem("u_done")
            sc_s = sch.sem("sc_s")
            scan_done = sch.sem("scan_done")
            hbf_done = sch.sem("hbf_done")
            v_done = sch.sem("v_done")
            vtm_done = sch.sem("vtm_done")
            wg_used = sch.sem("wg_used")
            wo_used = sch.sem("wo_used")
            sil_rdy = sch.sem("sil_rdy")
            ff1_rdy = sch.sem("ff1_rdy")
            smm_done = sch.sem("smm_done")
            s_ev = sch.sem("s_ev")
            s_st = sch.sem("s_st")
            xn_st = sch.sem("xn_st")
            pw_done = sch.sem("pw_done")
            hg_done = sch.sem("hg_done")
            xn_cp = sch.sem("xn_cp")
            vch = sch.sem("vch")

            h32hist = {}
            ctr = dict(jglob=0, gsc=0, rms=0, wgvchunk=0, wochunk=0,
                       tau_real=0, lnv={}, xnspill=0, sev=0)

            def emit_ln(jg, row0):
                js = jg % 2
                xt, lt = x_tm[js], lnt[js]
                wv = ((xfree, jg - 1),) if jg >= 2 else ()
                s_dma(xt[:], xs_d[row0:row0 + 128, :], waits=wv, inc=xq[js])
                xv = xq[js].v
                lnwar = ((ln_vs, ctr["lnv"].get(jg - 2, 0)),) if jg >= 2 else ()

                def v_red(e, xt=xt, lt=lt):
                    return e.reduce_sum(lt[:, 0:1], xt[:], axis=AXX)
                sch.emit("vector", v_red, waits=((xq[js], xv),),
                         incs=((ln_vs, 1),))
                rv = ln_vs.v

                def s_m(e, lt=lt):
                    return e.mul(lt[:, 1:2], lt[:, 0:1], 1.0 / D)
                sch.emit("scalar", s_m, waits=((ln_vs, rv),) + lnwar,
                         incs=((ln_sv, 1),))

                def s_sq(e, xt=xt, lt=lt):
                    return e.activation(lnsq[:], xt[:], AF.Square,
                                        accum_out=lt[:, 2:3])
                sch.emit("scalar", s_sq, waits=((xq[js], xv),),
                         incs=((ln_sv, 1),))

                def s_msq(e, lt=lt):
                    return e.square(lt[:, 3:4], lt[:, 1:2])
                sch.emit("scalar", s_msq, incs=((ln_sv, 1),))
                mv = ln_sv.v

                def v_var2(e, lt=lt):
                    return e.scalar_tensor_tensor(
                        lt[:, 4:5], lt[:, 2:3], 1.0 / D, lt[:, 3:4],
                        op0=ALU.mult, op1=ALU.subtract)
                sch.emit("vector", v_var2, waits=((ln_sv, mv),),
                         incs=((ln_vs, 1),))
                vv = ln_vs.v

                def s_ln(e, lt=lt):
                    return e.activation(lt[:, 5:6], lt[:, 4:5], AF.Ln,
                                        bias=eps_ln[:, 0:1])
                sch.emit("scalar", s_ln, waits=((ln_vs, vv),))

                def s_exp(e, lt=lt):
                    return e.activation(lt[:, 6:7], lt[:, 5:6], AF.Exp,
                                        scale=-0.5)
                sch.emit("scalar", s_exp, incs=((ln_sv, 1),))
                ev = ln_sv.v

                def v_ap(e, xt=xt, lt=lt):
                    return e.tensor_scalar(xt[:], xt[:], lt[:, 1:2], lt[:, 6:7],
                                           op0=ALU.subtract, op1=ALU.mult)
                sch.emit("vector", v_ap, waits=((ln_sv, ev),),
                         incs=((ln_vs, 1),))
                ctr["lnv"][jg] = ln_vs.v
                return ln_vs.v

            def emit_transpose_j(jg, dstcol, lnv):
                js = jg % 2
                xt = x_tm[js]
                for grp in range(2):
                    slot = P_tr
                    prev = slot.reads

                    def tf(e, xt=xt, grp=grp, s=slot):
                        last = None
                        for cc in range(4):
                            c = grp * 4 + cc
                            last = e.transpose(
                                s.ap[:, cc * 128:(cc + 1) * 128],
                                xt[:, c * 128:(c + 1) * 128], ident[:])
                        return last
                    tw = [(ln_vs, lnv), (slot.free, prev)]
                    sch.emit("tensor", tf, waits=tuple(tw))
                    sch.tmark(((slot.prod, 1),))
                    pv = slot.prod.v

                    def s_ev_f(e, grp=grp, s=slot, dstcol=dstcol):
                        dst = bass.AP(act32, grp * 4 * 512 + dstcol,
                                      [[CT * 512, 128], [512, 4], [1, 128]])
                        return e.copy(dst, s.ap[:].rearrange(
                            "p (c n) -> p c n", n=128))
                    sch.emit("scalar", s_ev_f, waits=((slot.prod, pv),),
                             incs=((slot.free, 1),))
                    slot.reads += 1
                sch.emit("tensor", lambda e: e.drain())
                sch.tmark(((xfree, 1),))
                return slot.free.v

            def emit_rms(gam, dst, dst_stride, dst_off, width, src_rdy,
                         sq_extra=(), ap_extra=()):
                call = ctr["rms"]
                ctr["rms"] += 1
                slot = P_row
                prev = slot.reads
                sqv = []
                for c in range(CT):
                    gsi = call * CT + c
                    war = ((stats_mm, gsi - 1),) if gsi >= 2 else ()

                    def s_sq(e, c=c, width=width, gsi=gsi):
                        return e.activation(
                            scrq[:, (gsi % 2) * 512:(gsi % 2) * 512 + width],
                            act32[:, c * 512:c * 512 + width], AF.Square)
                    sch.emit("scalar", s_sq,
                             waits=tuple(src_rdy) + tuple(sq_extra) + war,
                             incs=((sq_done, 1),))
                    sqv.append(sq_done.v)

                    def tmf(e, c=c, width=width, gsi=gsi, s=slot):
                        return e.matmul(
                            s.ap[0:1, 0:width], onesk[:],
                            scrq[:, (gsi % 2) * 512:(gsi % 2) * 512 + width],
                            start=(c == 0), stop=(c == CT - 1))
                    w = [(sq_done, sqv[-1])]
                    if c == 0:
                        w.append((slot.free, prev))
                    if c == CT - 1:
                        sch.emit("tensor", tmf, waits=tuple(w))
                        sch.tmark(((slot.prod, 1),))
                        sch.tmark(((stats_mm, 1),))
                    else:
                        sch.emit("tensor", tmf, waits=tuple(w),
                                 incs=((stats_mm, 1),))
                pv = slot.prod.v

                def s_lnr(e, width=width, s=slot):
                    return e.activation(rrow[0:1, 0:width], s.ap[0:1, 0:width],
                                        AF.Ln, bias=eps_rms[0:1, 0:1])
                sch.emit("scalar", s_lnr, waits=((slot.prod, pv),),
                         incs=((slot.free, 1),))
                slot.reads += 1

                def s_expr(e, width=width):
                    return e.activation(rrow[0:1, 0:width], rrow[0:1, 0:width],
                                        AF.Exp, scale=-0.5, bias=ln32t[0:1, 0:1])
                sch.emit("scalar", s_expr, incs=((rr_done, 1),))
                rrv = rr_done.v

                bslot = P_bc
                prevb = bslot.reads

                def tbf(e, width=width, s=bslot):
                    return e.matmul(s.ap[:, 0:width], onesf[:],
                                    rrow[0:1, 0:width], start=True, stop=True)
                sch.emit("tensor", tbf,
                         waits=((rr_done, rrv), (bslot.free, prevb)))
                sch.tmark(((bslot.prod, 1),))
                pbv = bslot.prod.v
                for c in range(CT):
                    def v_ap(e, c=c, dst=dst, gam=gam, width=width, s=bslot,
                             dst_stride=dst_stride, dst_off=dst_off):
                        return e.scalar_tensor_tensor(
                            dst[:, c * dst_stride + dst_off:
                                c * dst_stride + dst_off + width],
                            act32[:, c * 512:c * 512 + width],
                            gam[:, c:c + 1], s.ap[:, 0:width],
                            op0=ALU.mult, op1=ALU.mult)
                    incs = []
                    if c == CT - 1:
                        incs = [(bslot.free, 1)]
                    sch.emit("vector", v_ap,
                             waits=((bslot.prod, pbv),) + tuple(ap_extra),
                             incs=tuple(incs))
                bslot.reads += 1
                return sqv, bslot.free.v

            lnv = emit_ln(0, 0)
            trv = emit_transpose_j(0, 0, lnv)
            emit_rms(gc1, c_ext, 515, 387, 128, ((P_tr.free, trv),),
                     ap_extra=((gam_done, 3),))

            front_out = {}
            conv_prev = [()]

            def emit_front(tau):
                real = tau >= NWT
                base = 128 + tau * TAU
                for j in range(4):
                    ctr["jglob"] += 1
                    jg = ctr["jglob"]
                    lnv = emit_ln(jg, base + j * 128)
                    xrv = emit_transpose_j(jg, j * 128, lnv)

                if real:
                    for c in range(CT):
                        idx = ctr["xnspill"]
                        ctr["xnspill"] += 1
                        sl = idx % 2
                        war = ((xn_st, 16 * (idx - 1)),) if idx >= 2 else ()

                        def v_cp(e, c=c, sl=sl):
                            return e.tensor_copy(
                                xn_stg[sl][:],
                                act32[:, c * 512:(c + 1) * 512])
                        sch.emit("vector", v_cp,
                                 waits=((P_tr.free, xrv),) + war,
                                 incs=((xn_cp, 1),))
                        s_dma(xn_dram[c, :, (tau - NWT) * 512:(tau - NWT + 1) * 512],
                              xn_stg[sl][:], waits=((xn_cp, xn_cp.v),),
                              inc=xn_st)

                for c in range(CT):
                    def v_halo(e, c=c):
                        return e.tensor_copy(c_ext[:, c * 515:c * 515 + 3],
                                             c_ext[:, c * 515 + 512:c * 515 + 515])
                    hw = conv_prev[0] if c == 0 else ()
                    sch.emit("vector", v_halo, waits=hw)
                sqc, apc = emit_rms(gc1, c_ext, 515, 3, 512,
                                    ((P_tr.free, xrv),))

                front_out[tau] = (sqc, apc)

            def emit_conv(tau):
                sqc, apc = front_out[tau]
                hgv0 = hg_done.v
                for c in range(CT):
                    slot = pnext()
                    mats = [(dwd_bf[:, (c * 4 + k) * 128:(c * 4 + k + 1) * 128],
                             c_ext[:, c * 515 + k:c * 515 + k + 512])
                            for k in range(4)]

                    def cv_ev(e, c=c, s=slot):
                        return e.copy(bfA[:, c * 512:(c + 1) * 512], s.ap[:])
                    mm_group(sch, slot, slot.ap[:], mats,
                             [("scalar", cv_ev,
                               ((hg_done, hgv0),) if c == 0 else ())],
                             extra_waits=((P_bc.free, apc),) if c == 0 else (),
                             defer_prod=(c < CT - 1))
                cw = tuple((pp.free, pp.free.v) for pp in P)
                conv_prev[0] = cw
                return cw

            pending_s = [None]

            def emit_rest(tau, hoist_cb):
                real = tau >= NWT
                sqc, apc = front_out[tau]
                box = {}

                def emit_convpw():
                    cw = emit_conv(tau)
                    for m in range(CT):
                        slot = pnext()
                        mats = [(pwT[:, k * D + m * 128:k * D + (m + 1) * 128],
                                 bfA[:, k * 512:(k + 1) * 512])
                                for k in range(CT)]

                        def u_rd(e, m=m, s=slot):
                            return e.scalar_tensor_tensor(
                                act32[:, m * 512:(m + 1) * 512], s.ap[:],
                                bcomb[:, m:m + 1],
                                act32[:, m * 512:(m + 1) * 512],
                                op0=ALU.add, op1=ALU.add)
                        mm_group(sch, slot, slot.ap[:], mats,
                                 [("vector", u_rd, ((sq_done, sqc[m]),))],
                                 extra_waits=cw if m == 0 else (),
                                 defer_prod=(m < CT - 1))
                    sch.tmark(((pw_done, 1),))
                    box["pwv"] = pw_done.v
                    box["uw"] = tuple((pp.free, pp.free.v) for pp in P)
                if pending_s[0] is not None:
                    pending_s[0](emit_convpw)
                    pending_s[0] = None
                else:
                    emit_convpw()
                pwv = box["pwv"]
                uw = box["uw"]
                if taps and real:
                    for c in range(CT):
                        s_dma(tap_aps["tap_u"][c, :, (tau - NWT) * 512:(tau - NWT + 1) * 512],
                              act32[:, c * 512:(c + 1) * 512],
                              waits=uw)

                sqg, apg = emit_rms(gg1, bfA, 512, 0, 512, uw,
                                    sq_extra=((pw_done, pwv),),
                                    ap_extra=((pw_done, pwv),))

                def emit_wout_head(h):
                    hbv = hbf_done.v
                    for m in (2 * h, 2 * h + 1):
                        slot = pnext()
                        mats = [(wout[:, (3 * h + kk) * 256 + (m % 2) * 128:
                                     (3 * h + kk) * 256 + (m % 2 + 1) * 128],
                                 h_bf[:, (3 * h + kk) * 512:(3 * h + kk + 1) * 512])
                                for kk in range(3)]

                        def v_rd(e, m=m, s=slot):
                            return e.tensor_tensor(
                                act32[:, m * 512:(m + 1) * 512], s.ap[:],
                                act32[:, m * 512:(m + 1) * 512], op=ALU.add)
                        mm_group(sch, slot, slot.ap[:], mats,
                                 [("vector", v_rd, ((sq_done, sqg[m]),))],
                                 extra_waits=((hbf_done, hbv),),
                                 defer_prod=(m < 2 * h + 1 or h < 3))

                for h in range(4):
                    for i in range(3):
                        gsc = ctr["gsc"]
                        ctr["gsc"] += 1
                        so = (gsc % 2) * 512
                        war = ((scan_done, gsc - 1),) if gsc >= 2 else ()
                        hid = pnext()
                        gat = pnext()
                        hmats = [(whg[:, (2 * h + kk) * 768 + i * 128:
                                      (2 * h + kk) * 768 + (i + 1) * 128],
                                  bfA[:, (2 * h + kk) * 512:(2 * h + kk + 1) * 512])
                                 for kk in range(2)]
                        gmats = [(whg[:, (2 * h + kk) * 768 + (3 + i) * 128:
                                      (2 * h + kk) * 768 + (4 + i) * 128],
                                  bfA[:, (2 * h + kk) * 512:(2 * h + kk + 1) * 512])
                                 for kk in range(2)]

                        def r_sg(e, s=hid, so=so):
                            return e.activation(sg32[:, so:so + 512], s.ap[:],
                                                AF.Sigmoid)

                        def r_mk(e, s=hid, so=so):
                            return e.scalar_tensor_tensor(
                                gh32[:, so:so + 512], s.ap[:], 0.5,
                                sg32[:, so:so + 512], op0=ALU.add, op1=ALU.max)

                        def r_a(e, s=gat, so=so):
                            return e.activation(a32[:, so:so + 512], s.ap[:],
                                                AF.Sigmoid, scale=-1.0)

                        def r_z(e, s=gat, so=so):
                            return e.activation(z32[:, so:so + 512], s.ap[:],
                                                AF.Sigmoid)
                        mm_group(sch, hid, hid.ap[:], hmats,
                                 [("scalar", r_sg, war),
                                  ("vector", r_mk,
                                   war + ((hid.free, hid.free.v + 1),))],
                                 extra_waits=((P_bc.free, apg),),
                                 defer_prod=True)
                        hfv = hid.free.v
                        mm_group(sch, gat, gat.ap[:], gmats,
                                 [("scalar", r_a), ("scalar", r_z)],
                                 defer_prod=(h * 3 + i < 11))
                        gfv = gat.free.v

                        def v_b(e, so=so):
                            return e.tensor_tensor(gh32[:, so:so + 512],
                                                   z32[:, so:so + 512],
                                                   gh32[:, so:so + 512],
                                                   op=ALU.mult)
                        sch.emit("vector", v_b, waits=((gat.free, gfv),))
                        sc = h * 3 + i

                        def v_scan(e, so=so, sc=sc):
                            return e.tensor_tensor_scan(
                                h32[:, 0:512], a32[:, so:so + 512],
                                gh32[:, so:so + 512],
                                carry[:, sc:sc + 1], op0=ALU.mult, op1=ALU.add)
                        sch.emit("vector", v_scan, incs=((vch, 1),))
                        vchv = vch.v

                        def v_cr(e, sc=sc):
                            return e.tensor_copy(carry[:, sc:sc + 1],
                                                 h32[:, 511:512])
                        sch.emit("vector", v_cr, waits=((vch, vchv),),
                                 incs=((scan_done, 1),))
                        if real:
                            def v_hbf(e, sc=sc):
                                return e.tensor_copy(
                                    h_bf[:, sc * 512:(sc + 1) * 512],
                                    h32[:, 0:512])
                            sch.emit("vector", v_hbf, incs=((hbf_done, 1),))
                sch.tmark(((hg_done, 1),))
                hgv = hg_done.v
                if taps and real:
                    for sc in range(SCT):
                        g_dma(tap_aps["tap_h"][sc, :, (tau - NWT) * 512:(tau - NWT + 1) * 512],
                              h_bf[:, sc * 512:(sc + 1) * 512],
                              waits=((hbf_done, hbf_done.v),))

                if tau == NWT - 1:
                    def v_zsc(e):
                        return e.tensor_scalar(carry[:], carry[:],
                                               zsc[:, 0:1], None, op0=ALU.mult)
                    sch.emit("vector", v_zsc)

                if not real:
                    return

                for h in range(4):
                    emit_wout_head(h)
                vw = tuple((pp.free, pp.free.v) for pp in P)

                sqf, apf = emit_rms(gf1, bfN, 512, 0, 512, vw,
                                    sq_extra=((hg_done, hgv),),
                                    ap_extra=((hg_done, hgv),))

                if taps and tau == NWT:
                    for c in range(CT):
                        g_dma(tap_aps["tap_n2"][:, c * 512:(c + 1) * 512],
                              bfA[:, c * 512:(c + 1) * 512],
                              waits=((P_bc.free, apf),))

                for tb in range(4):
                    for grp in range(2):
                        slot = P_tr
                        prev = slot.reads

                        def tf(e, tb=tb, grp=grp, s=slot):
                            last = None
                            for cc in range(4):
                                c = grp * 4 + cc
                                last = e.transpose(
                                    s.ap[:, cc * 128:(cc + 1) * 128],
                                    act32[:, c * 512 + tb * 128:
                                          c * 512 + (tb + 1) * 128], ident[:])
                            return last
                        sch.emit("tensor", tf,
                                 waits=vw + ((slot.free, prev),))
                        sch.tmark(((slot.prod, 1),))
                        pv = slot.prod.v

                        def sv_ev(e, tb=tb, grp=grp, s=slot):
                            dst = v_tm[:, tb * D + grp * 4 * 128:
                                       tb * D + (grp + 1) * 4 * 128]
                            return e.copy(dst, s.ap[:, 0:512])
                        sch.emit("scalar", sv_ev, waits=((slot.prod, pv),),
                                 incs=((slot.free, 1),))
                        slot.reads += 1
                vtmv = P_tr.free.v
                sch.emit("tensor", lambda e: e.drain())

                rt = ctr["tau_real"]
                ctr["tau_real"] += 1
                ffmarks = {}
                for mfi in range(FFT):
                    mf = mfi
                    ck = ctr["wgvchunk"]
                    ctr["wgvchunk"] += 1
                    rs = ck % 2
                    wwar = ((wg_used, ck - 1),) if ck >= 2 else ()
                    g_dma(wgs[:, rs * CT * 128:(rs + 1) * CT * 128]
                          .rearrange("p (k m) -> p k m", m=128),
                          bass.AP(wgate_b.tensor, mf * 128,
                                  [[FFP, 128], [128 * FFP, CT], [1, 128]]),
                          waits=wwar + (((ld, pc_all),) if ck < 2 else ()))
                    ckv = g_dma(wvs[:, rs * CT * 128:(rs + 1) * CT * 128]
                                .rearrange("p (k m) -> p k m", m=128),
                                bass.AP(wval_b.tensor, mf * 128,
                                        [[FFP, 128], [128 * FFP, CT], [1, 128]]),
                                waits=())
                    slg = pnext()
                    slv = pnext()
                    gmats = [(wgs[:, rs * CT * 128 + k * 128:
                                  rs * CT * 128 + (k + 1) * 128],
                              bfN[:, k * 512:(k + 1) * 512]) for k in range(CT)]
                    vmats = [(wvs[:, rs * CT * 128 + k * 128:
                                  rs * CT * 128 + (k + 1) * 128],
                              bfN[:, k * 512:(k + 1) * 512]) for k in range(CT)]

                    def r_sil(e, mf=mf, s=slg):
                        return e.activation(
                            sil[:, (mf % 2) * 512:(mf % 2) * 512 + 512],
                            s.ap[:], AF.Silu)
                    sil_war = (ffmarks[mf - 2],) if mfi >= 2 else ()
                    mm_group(sch, slg, slg.ap[:], gmats,
                             [("scalar", r_sil, sil_war)],
                             extra_waits=((ld, ckv), (P_bc.free, apf)),
                             defer_prod=True)
                    silv = (slg.free, slg.free.v)

                    def r_ff1(e, mf=mf, s=slv):
                        return e.tensor_tensor(
                            ff1[:, mf * 512:(mf + 1) * 512],
                            sil[:, (mf % 2) * 512:(mf % 2) * 512 + 512],
                            s.ap[:], op=ALU.mult)
                    ffw_w = [silv]
                    if rt > 0 and mf == 0:
                        ffw_w.append((smm_done, rt))
                    mm_group(sch, slv, slv.ap[:], vmats,
                             [("vector", r_ff1, tuple(ffw_w))],
                             defer_prod=True)
                    ffmarks[mf] = (slv.free, slv.free.v)
                    sch.tmark(((wg_used, 1),))
                ffw_pairs = {}
                for sm, vv_ in ffmarks.values():
                    ffw_pairs[id(sm)] = (sm, vv_)
                ffw_pairs = tuple(ffw_pairs.values())

                if taps and tau == NWT:
                    for mf in range(FFT):
                        g_dma(tap_aps["tap_ff1"][:, mf * 512:(mf + 1) * 512],
                              ff1[:, mf * 512:(mf + 1) * 512],
                              waits=ffw_pairs)

                hoist_cb()

                def emit_sphase(mid_cb, tau=tau, ffw_pairs=ffw_pairs,
                                vtmv=vtmv):
                  if SKIP_SPHASE:
                      if mid_cb is not None:
                          mid_cb()
                      return
                  for dh in range(2):
                      if dh == 1 and mid_cb is not None:
                          mid_cb()
                      chunk_tgt = []
                      for k in range(FFT):
                          ok = ctr["wochunk"]
                          ctr["wochunk"] += 1
                          owar = ((wo_used, ok - 3),) if ok >= 4 else ()
                          tgt = g_dma(wos[:, (ok % 4) * D:(ok % 4 + 1) * D],
                                      wffo_b[k * 128:(k + 1) * 128, :],
                                      waits=owar +
                                      (((ld, pc_all),) if ok < 4 else ()))
                          chunk_tgt.append((ok % 4, tgt))
                      for k in range(FFT):
                          rs, tgt = chunk_tgt[k]
                          for tb in range(4):
                              slot = P[tb]
                              w = [(ld, tgt)]
                              if k == 0:
                                  w.append((slot.free, slot.reads))
                                  if dh == 0:
                                      w.extend(ffw_pairs)
                              incs = []
                              if tb == 3 and k < FFT - 1:
                                  incs.append((wo_used, 1))
                              if not JUNK_GROUP and k == FFT - 1:
                                  incs.append((slot.prod, 1))

                              def tmm(e, k=k, tb=tb, rs=rs, dh=dh, s=slot):
                                  return e.matmul(
                                      s.ap[:],
                                      ff1[:, k * 512 + tb * 128:
                                          k * 512 + (tb + 1) * 128],
                                      wos[:, rs * D + dh * 512:
                                          rs * D + (dh + 1) * 512],
                                      start=(k == 0), stop=(k == FFT - 1))
                              sch.emit("tensor", tmm, waits=tuple(w),
                                       incs=tuple(incs))
                      sch.tmark(((wo_used, 1),))
                      if JUNK_GROUP:
                          for tb in range(4):
                              sch.tmark(((P[tb].prod, 1),))
                      for tb in range(4):
                          slot = P[tb]
                          pv = slot.prod.v
                          sidx = ctr["sev"]
                          ctr["sev"] += 1
                          ssl = sidx % 2
                          swar = ((s_st, 16 * (sidx - 1)),) if sidx >= 2 else ()

                          def s_rd(e, tb=tb, dh=dh, ssl=ssl, s=slot):
                              return e.tensor_tensor(
                                  s_sp[ssl][:, 0:512], s.ap[:],
                                  v_tm[:, tb * D + dh * 512:
                                       tb * D + (dh + 1) * 512],
                                  op=ALU.add)
                          sch.emit("vector", s_rd,
                                   waits=((slot.prod, pv), (P_tr.free, vtmv))
                                   + swar,
                                   incs=((slot.free, 1),))
                          slot.reads += 1
                          sev = (slot.free, slot.free.v)
                          r0 = (tau - NWT) * 512 + tb * 128
                          dst = bass.AP(s_dram.tensor, r0 * D + dh * 512,
                                        [[D, 128], [1, 512]])
                          s_dma(dst, s_sp[ssl][:, 0:512],
                                waits=(sev,), inc=s_st)
                          if taps:
                              tdst = bass.AP(tap_aps["tap_s"].tensor,
                                             r0 * D + dh * 512,
                                             [[D, 128], [1, 512]])
                              s_dma(tdst, s_sp[ssl][:, 0:512],
                                    waits=(sev,))
                  sch.tmark(((smm_done, 1),))
                pending_s[0] = emit_sphase

            emit_front(0)
            for tau in range(NTAU):
                hoisted = [False]

                def _cb(tau=tau, hoisted=hoisted):
                    if tau + 1 < NTAU:
                        emit_front(tau + 1)
                    hoisted[0] = True
                emit_rest(tau, _cb)
                if not hoisted[0] and tau + 1 < NTAU:
                    emit_front(tau + 1)
            if pending_s[0] is not None:
                pending_s[0](None)
                pending_s[0] = None

            sch.flush()

        if SKIP_MOE:
            def fin_dummy(e):
                return e.nop()
            sch.emit("sync", fin_dummy, waits=((s_st, s_st.v),))
            sch.flush()
            return nc
        with ExitStack() as moe:
            qsb = lambda name, shape, dt=F32: moe.enter_context(
                nc.sbuf_tensor(name, shape, dt))
            h1h = qsb("h1h", [128, 16 * PASS_T], BF16)
            w1s = qsb("w1s", [128, 2 * CT * 512], BF16)
            w2h = qsb("w2h", [128, 16 * D], BF16)
            ystg = qsb("ystg", [128, 8 * D])
            g_acc = qsb("g_acc", [128, 8 * D])
            xnm2 = [qsb(f"xnm{i}", [128, CT * PASS_T], BF16)
                    for i in range(2)]
            w_all = qsb("w_all", [128, 64])
            wgsb = qsb("wgsb", [128, 64], BF16)
            bg_bc = qsb("bg_bc", [128, 8])
            b1c = qsb("b1c_sb", [128, 256])
            b2s = qsb("b2s_sb", [1, 2048], BF16)
            lgt = qsb("lgt", [128, 2 * 8])
            elg = qsb("elg", [128, 2 * 8])
            pst = qsb("pst", [128, 2 * 8])
            s_rl = [qsb(f"s_rl{i}", [128, D]) for i in range(2)]
            x0_t = [qsb(f"x0_t{i}", [128, D]) for i in range(2)]
            o_sb = [qsb(f"o_sb{i}", [128, D]) for i in range(2)]

            P_lg = PSlot(sch, moe.enter_context(
                nc.psum_tensor("plg", [128, 512], F32)), "plg")
            xnm_q = sch.sem("xnm_q")
            w1_used = sch.sem("w1_used")
            w2_used = sch.sem("w2_used")
            h1_rdy = sch.sem("h1_rdy")
            ystA = sch.sem("ystA")
            gacc_done = sch.sem("gacc_done")
            moe_vs = sch.sem("moe_vs")
            moe_sv = sch.sem("moe_sv")
            pre_done = sch.sem("pre_done")
            gsil = sch.sem("gsil")
            srl_q = sch.sem("srl_q")
            x0_q = sch.sem("x0_q")
            fin_v = sch.sem("fin_v")
            out_q = sch.sem("out_q")
            mch = sch.sem("mch")

            g_dma(wgsb[:], moewg_d)
            g_dma(b1c[:], b1c_d)
            bgv = g_dma(bg_bc[0:1, 0:8], bg_d)
            slot = pnext()

            def bgf(e, s=slot):
                return e.matmul(s.ap[:, 0:8], onesf[:], bg_bc[0:1, 0:8],
                                start=True, stop=True)
            sch.emit("tensor", bgf,
                     waits=((ld, bgv), (slot.free, slot.reads)))
            sch.tmark(((slot.prod, 1),))

            def bgr(e, s=slot):
                return e.tensor_copy(bg_bc[:], s.ap[:, 0:8])
            sch.emit("vector", bgr, waits=((slot.prod, slot.prod.v),),
                     incs=((slot.free, 1),))
            slot.reads += 1

            xnvs = []
            for p in range(NPASS):
                s_dma(xnm2[p % 2][:].rearrange("p (c t) -> p c t", t=PASS_T),
                      bass.AP(xn_dram.tensor, p * PASS_T,
                              [[T, 128], [128 * T, CT], [1, PASS_T]]),
                      waits=(), inc=xnm_q)
                xnvs.append(xnm_q.v)

            for p in range(NPASS):
                xnm = xnm2[p % 2]
                xnv = xnvs[p]


                def emit_preamble(p=p, xnv=xnv):
                  for tb in range(8):
                      sl = tb % 2
                      slot = P_lg
                      mats = [(xnm[:, k * PASS_T + tb * 128:
                                   k * PASS_T + (tb + 1) * 128],
                               wgsb[:, k * 8:(k + 1) * 8]) for k in range(CT)]
                      prewar = ((pre_done, pre_done.v - 1),) if (p * 8 + tb) >= 2 \
                          else ()

                      def r_lg(e, sl=sl, s=slot):
                          return e.tensor_tensor(lgt[:, sl * 8:sl * 8 + 8],
                                                 s.ap[:, 0:8], bg_bc[:],
                                                 op=ALU.add)
                      mm_group(sch, slot, slot.ap[:, 0:8], mats,
                               [("vector", r_lg, prewar)],
                               extra_waits=((xnm_q, xnv),))
                      lgv = (slot.free, slot.free.v)

                      def v_m1(e, sl=sl):
                          return e.reduce_max(pst[:, sl * 8:sl * 8 + 1],
                                              lgt[:, sl * 8:sl * 8 + 8], axis=AXX)
                      sch.emit("vector", v_m1, waits=(lgv,), incs=((mch, 1),))
                      mchv = mch.v

                      def v_nm1(e, sl=sl):
                          return e.tensor_scalar(pst[:, sl * 8 + 1:sl * 8 + 2],
                                                 pst[:, sl * 8:sl * 8 + 1],
                                                 -1.0, None, op0=ALU.mult)
                      sch.emit("vector", v_nm1, waits=((mch, mchv),),
                               incs=((moe_vs, 1),))
                      mv = moe_vs.v

                      def s_exp(e, sl=sl):
                          return e.activation(
                              elg[:, sl * 8:sl * 8 + 8],
                              lgt[:, sl * 8:sl * 8 + 8],
                              AF.Exp, bias=pst[:, sl * 8 + 1:sl * 8 + 2],
                              accum_out=pst[:, sl * 8 + 2:sl * 8 + 3])
                      exw = ((moe_vs, mv),) + prewar
                      if p == 0 and tb == 0:
                          exw = exw + ((ld, bgv),)
                      sch.emit("scalar", s_exp, waits=exw,
                               incs=((moe_sv, 1),))
                      sv = moe_sv.v

                      chain = []

                      def estep(fn, waits=(), incs=None):
                          sch.emit("vector", fn,
                                   waits=waits + ((mch, mch.v),),
                                   incs=incs if incs is not None
                                   else ((mch, 1),))
                      estep(lambda e, sl=sl: e.reciprocal(
                          pst[:, sl * 8 + 3:sl * 8 + 4],
                          pst[:, sl * 8 + 2:sl * 8 + 3]), waits=((moe_sv, sv),))
                      estep(lambda e, sl=sl: e.tensor_scalar(
                          elg[:, sl * 8:sl * 8 + 8], elg[:, sl * 8:sl * 8 + 8],
                          pst[:, sl * 8 + 3:sl * 8 + 4], None, op0=ALU.mult))
                      estep(lambda e, sl=sl: e.reduce_max(
                          pst[:, sl * 8 + 4:sl * 8 + 5],
                          elg[:, sl * 8:sl * 8 + 8], axis=AXX))
                      estep(lambda e, sl=sl: e.tensor_scalar(
                          lgt[:, sl * 8:sl * 8 + 8], elg[:, sl * 8:sl * 8 + 8],
                          pst[:, sl * 8 + 4:sl * 8 + 5], None, op0=ALU.is_lt))
                      estep(lambda e, sl=sl: e.tensor_tensor(
                          lgt[:, sl * 8:sl * 8 + 8], lgt[:, sl * 8:sl * 8 + 8],
                          elg[:, sl * 8:sl * 8 + 8], op=ALU.mult))
                      estep(lambda e, sl=sl: e.reduce_max(
                          pst[:, sl * 8 + 5:sl * 8 + 6],
                          lgt[:, sl * 8:sl * 8 + 8], axis=AXX))
                      estep(lambda e, sl=sl: e.tensor_scalar(
                          lgt[:, sl * 8:sl * 8 + 8], elg[:, sl * 8:sl * 8 + 8],
                          pst[:, sl * 8 + 5:sl * 8 + 6], None, op0=ALU.is_ge))
                      estep(lambda e, sl=sl, tb=tb: e.tensor_tensor(
                          w_all[:, tb * 8:(tb + 1) * 8],
                          elg[:, sl * 8:sl * 8 + 8],
                          lgt[:, sl * 8:sl * 8 + 8], op=ALU.mult),
                          incs=((pre_done, 1),))

                  if taps:
                      prew0 = pre_done.v
                      s_dma(tap_aps["tap_w"][:, p * 64:(p + 1) * 64], w_all[:],
                            waits=((pre_done, prew0),))
                for ex in range(NEXP):
                    exg = p * NEXP + ex
                    for half in range(2):
                        hb = half * 2048
                        gq = (p * NEXP + ex) * 2 + half
                        if half == 1:
                            b2war = ((w2_used, 2 * exg - 2),) \
                                if exg >= 2 else ()
                            g_dma(b2s[0:1, (exg % 2) * D:(exg % 2 + 1) * D],
                                  b2r_d[0:1, ex * D:(ex + 1) * D],
                                  waits=b2war)
                        for mc in range(4):
                            gmc = gq * 4 + mc
                            war = ((w1_used, gmc - 1),) if gmc >= 2 else ()
                            w1v = g_dma(
                                w1s[:, (gmc % 2) * CT * 512:
                                    (gmc % 2 + 1) * CT * 512]
                                .rearrange("p (k m) -> p k m", m=512),
                                bass.AP(w1_d.tensor,
                                        ex * D * MH + hb + mc * 512,
                                        [[MH, 128], [128 * MH, CT], [1, 512]]),
                                waits=war)
                            qwar = ((w2_used, gq),) if gq >= 1 else ()
                            w2v = g_dma(
                                w2h[:, mc * 4 * D:(mc + 1) * 4 * D]
                                .rearrange("p (k m) -> p k m", m=D),
                                bass.AP(w2_d.tensor,
                                        ex * MH * D + (hb + mc * 512) * D,
                                        [[D, 128], [128 * D, 4], [1, D]]),
                                waits=qwar)
                            for mtl in range(4):
                                mth = mc * 4 + mtl
                                for nn in range(2):
                                    slot = pnext()
                                    mats = [(w1s[:, (gmc % 2) * CT * 512 +
                                                 k * 512 + mtl * 128:
                                                 (gmc % 2) * CT * 512 +
                                                 k * 512 + (mtl + 1) * 128],
                                             xnm[:, k * PASS_T + nn * 512:
                                                 k * PASS_T + (nn + 1) * 512])
                                            for k in range(CT)]

                                    def r_gel(e, mth=mth, nn=nn, ex=ex,
                                              half=half, s=slot):
                                        return e.activation(
                                            h1h[:, mth * PASS_T + nn * 512:
                                                mth * PASS_T + (nn + 1) * 512],
                                            s.ap[:], AF.Gelu,
                                            bias=b1c[:, ex * 32 + half * 16 + mth:
                                                     ex * 32 + half * 16 + mth + 1])
                                    gel_w = ()
                                    if gq >= 1 and mc == 0 and mtl == 0 and nn == 0:
                                        gel_w = ((w2_used, gq),)
                                    mm_group(
                                        sch, slot, slot.ap[:], mats,
                                        [("scalar", r_gel, gel_w)],
                                        extra_waits=((ld, w1v), (xnm_q, xnv)),
                                        defer_prod=True)
                            sch.tmark(((w1_used, 1),))
                        h1w = tuple((pp.free, pp.free.v) for pp in P)
                        if ex == 0 and half == 0:
                            emit_preamble()

                        if taps and p == 0 and ex == 0 and half == 0:
                            for mt_ in range(16):
                                g_dma(tap_aps["tap_h1"][:, mt_ * PASS_T:
                                                        (mt_ + 1) * PASS_T],
                                      h1h[:, mt_ * PASS_T:(mt_ + 1) * PASS_T],
                                      waits=h1w)
                        for tb in range(8):
                            for dh in range(2):
                                slot = pnext()
                                mats = [(h1h[:, kt * PASS_T + tb * 128:
                                             kt * PASS_T + (tb + 1) * 128],
                                         w2h[:, kt * D + dh * 512:
                                             kt * D + (dh + 1) * 512])
                                        for kt in range(16)]
                                if half == 1:
                                    mats.append(
                                        (ones_bf[:],
                                         b2s[0:1, (exg % 2) * D + dh * 512:
                                             (exg % 2) * D + (dh + 1) * 512]))
                                if half == 0:
                                    def r_y(e, tb=tb, dh=dh, s=slot):
                                        return e.copy(
                                            ystg[:, tb * D + dh * 512:
                                                 tb * D + (dh + 1) * 512],
                                            s.ap[:])
                                    ywait = ((gacc_done, gacc_done.v),) \
                                        if (ex > 0 or p > 0) else ()
                                    mm_group(sch, slot, slot.ap[:], mats,
                                             [("scalar", r_y, ywait)],
                                             extra_waits=((ld, w2v),) + h1w,
                                             defer_prod=True)
                                else:
                                    def r_y2(e, tb=tb, dh=dh, s=slot):
                                        return e.tensor_tensor(
                                            ystg[:, tb * D + dh * 512:
                                                 tb * D + (dh + 1) * 512],
                                            s.ap[:],
                                            ystg[:, tb * D + dh * 512:
                                                 tb * D + (dh + 1) * 512],
                                            op=ALU.add)
                                    mm_group(sch, slot, slot.ap[:], mats,
                                             [("vector", r_y2, ystw)],
                                             extra_waits=((ld, w2v),) + h1w,
                                             defer_prod=True)
                        if half == 0:
                            ystw = tuple((pp.free, pp.free.v) for pp in P)
                        sch.tmark(((w2_used, 1),))
                    if taps and ex == 0:
                        yfw = tuple((pp.free, pp.free.v) for pp in P)
                        s_dma(tap_aps["tap_y"][:, p * 8192:(p + 1) * 8192],
                              ystg[:], waits=yfw)
                    prew = pre_done.v
                    for tb in range(8):
                        for dh in range(2):
                            def v_cmb(e, tb=tb, dh=dh, ex=ex):
                                if ex == 0:
                                    return e.tensor_scalar(
                                        g_acc[:, tb * D + dh * 512:
                                              tb * D + (dh + 1) * 512],
                                        ystg[:, tb * D + dh * 512:
                                             tb * D + (dh + 1) * 512],
                                        w_all[:, tb * 8:tb * 8 + 1],
                                        None, op0=ALU.mult)
                                return e.scalar_tensor_tensor(
                                    g_acc[:, tb * D + dh * 512:
                                          tb * D + (dh + 1) * 512],
                                    ystg[:, tb * D + dh * 512:
                                         tb * D + (dh + 1) * 512],
                                    w_all[:, tb * 8 + ex:tb * 8 + ex + 1],
                                    g_acc[:, tb * D + dh * 512:
                                          tb * D + (dh + 1) * 512],
                                    op0=ALU.mult, op1=ALU.add)
                            sch.emit("vector", v_cmb, waits=((pre_done, prew),),
                                     incs=((gacc_done, 1),))

                def emit_final(p=p, gav=gacc_done.v):
                  for tb in range(8):
                      def s_sil(e, tb=tb):
                          return e.activation(g_acc[:, tb * D:(tb + 1) * D],
                                              g_acc[:, tb * D:(tb + 1) * D],
                                              AF.Silu)
                      sch.emit("scalar", s_sil, waits=((gacc_done, gav),),
                               incs=((gsil, 1),))
                  gsv = gsil.v
                  if taps:
                      for tb in range(8):
                          s_dma(tap_aps["tap_g"][:, p * 8192 + tb * D:
                                                 p * 8192 + (tb + 1) * D],
                                g_acc[:, tb * D:(tb + 1) * D],
                                waits=((gsil, gsv),))
                  for tb in range(8):
                      gtb = p * 8 + tb
                      sl = tb % 2
                      war = ((fin_v, fin_v.v - 1),) if gtb >= 2 else ()
                      owar = ((out_q, 16 * (gtb - 1)),) if gtb >= 2 else ()
                      r0 = gtb * 128
                      s_dma(s_rl[sl][:], s_dram[r0:r0 + 128, :],
                            waits=((s_st, s_st.v),) + war, inc=srl_q)
                      s_dma(x0_t[sl][:],
                            xs_d[128 + WARM + r0:128 + WARM + r0 + 128, :],
                            waits=war, inc=x0_q)
                      sqv2 = srl_q.v
                      xqv2 = x0_q.v

                      def v_sg(e, sl=sl, tb=tb):
                          return e.scalar_tensor_tensor(
                              o_sb[sl][:], s_rl[sl][:], 1.0,
                              g_acc[:, tb * D:(tb + 1) * D],
                              op0=ALU.mult, op1=ALU.mult)
                      sch.emit("vector", v_sg,
                               waits=((srl_q, sqv2), (gsil, gsv)) + owar,
                               incs=((mch, 1),))
                      sgv = mch.v

                      def v_x0(e, sl=sl):
                          return e.tensor_tensor(o_sb[sl][:], o_sb[sl][:],
                                                 x0_t[sl][:], op=ALU.add)
                      sch.emit("vector", v_x0, waits=((x0_q, xqv2), (mch, sgv)),
                               incs=((fin_v, 1),))
                      fv = fin_v.v
                      s_dma(out_d[r0:r0 + 128, :], o_sb[sl][:],
                            waits=((fin_v, fv),), inc=out_q)

                emit_final()

            sch.emit("sync", lambda e: e.nop(), waits=((out_q, out_q.v),))
            sch.flush()

    return nc


_CACHE = {}


def _dwdiag(dw_w):
    dwv = np.ascontiguousarray(
        np.asarray(dw_w, np.float32)[:, 0, :]
        .reshape(8, 128, 4).transpose(1, 0, 2).reshape(128, 32))
    dd = np.zeros((128, 32 * 128), np.float32)
    p = np.arange(128)
    for ck in range(32):
        dd[p, ck * 128 + p] = dwv[:, ck]
    return dd


def _prep(inputs):
    x = np.asarray(inputs["L0_attn_x"], np.float32).reshape(L, D)
    slabs = []
    for i in range(NCORES):
        lo = 2048 * i - WARM - 3
        hi = 2048 * (i + 1)
        if lo < 0:
            slab = np.concatenate(
                [np.zeros((-lo, D), np.float32), x[0:hi]], 0)
        else:
            slab = x[lo:hi]
        slab = np.concatenate([np.zeros((125, D), np.float32), slab], 0)
        assert slab.shape[0] == SLAB
        slabs.append(np.ascontiguousarray(slab))

    def c128(v):
        return np.ascontiguousarray(
            np.asarray(v, np.float32).reshape(8, 128).T)

    wgate = np.zeros((D, FFP), np.float32)
    wgate[:, :2730] = np.asarray(inputs["L0_attn_w_gate"], np.float32)
    wval = np.zeros((D, FFP), np.float32)
    wval[:, :2730] = np.asarray(inputs["L0_attn_w_value"], np.float32)
    wffo = np.zeros((FFP, D), np.float32)
    wffo[:2730, :] = np.asarray(inputs["L0_attn_w_ff_out"], np.float32)
    common = {
        "pwT": np.ascontiguousarray(
            np.asarray(inputs["L0_attn_pw_w"], np.float32).T),
        "whg": np.ascontiguousarray(
            np.asarray(inputs["L0_attn_w_hg"], np.float32).reshape(D, 768)),
        "wout": np.ascontiguousarray(
            np.asarray(inputs["L0_attn_w_out"], np.float32).reshape(1536, 256)),
        "dwdiag": _dwdiag(inputs["L0_attn_dw_w"]),
        "dwb": c128(inputs["L0_attn_dw_b"]),
        "pwb": c128(inputs["L0_attn_pw_b"]),
        "gc": c128(inputs["L0_attn_conv_gamma"]),
        "gg": c128(inputs["L0_attn_gru_gamma"]),
        "gf": c128(inputs["L0_attn_ff_gamma"]),
        "wgate": wgate, "wval": wval, "wffo": wffo,
        "moewg": np.ascontiguousarray(
            np.asarray(inputs["L0_ffn_Wg"], np.float32)
            .reshape(8, 128, 8).transpose(1, 0, 2).reshape(128, 64)),
        "bgx": np.asarray(inputs["L0_ffn_bg"], np.float32).reshape(1, 8),
        "b1c": np.ascontiguousarray(
            np.asarray(inputs["L0_ffn_b1"], np.float32)
            .reshape(8, 32, 128).transpose(2, 0, 1).reshape(128, 256)),
        "b2r": np.ascontiguousarray(
            np.asarray(inputs["L0_ffn_b2"], np.float32).reshape(1, 8192)),
        "w1": np.ascontiguousarray(np.asarray(inputs["L0_ffn_W1"], np.float32)),
        "w2": np.ascontiguousarray(np.asarray(inputs["L0_ffn_W2"], np.float32)),
        "ident": np.eye(128, dtype=np.float32),
    }
    in_maps = []
    for i in range(NCORES):
        m = dict(common)
        m["xs"] = slabs[i]
        m["zsc"] = np.full((128, 1), 0.0 if i == 0 else 1.0, np.float32)
        in_maps.append(m)
    return in_maps


def kernel(**inputs):
    if "nc" not in _CACHE:
        _CACHE["nc"] = build_program(taps=TAPS)
    nc = _CACHE["nc"]
    in_maps = _prep(inputs)
    res = run_bass_kernel_spmd(nc, in_maps, core_ids=list(range(NCORES)))
    out = np.concatenate([np.asarray(r["out"], np.float32)
                          for r in res.results], 0)
    if TAPS:
        kernel.last_results = res
    return out.reshape(1, L, D)


# version 32
# speedup vs baseline: 1.0592x; 1.0592x over previous
import numpy as np
from contextlib import ExitStack

import concourse.bass as bass
import concourse.mybir as mybir
from concourse.bass_utils import run_bass_kernel_spmd

F32 = mybir.dt.float32
BF16 = mybir.dt.bfloat16
U8 = mybir.dt.uint8
AF = mybir.ActivationFunctionType
ALU = mybir.AluOpType
AXX = mybir.AxisListType.X

NCORES = 8
L = 16384
D = 1024
CT = 8
T = 2048
WARM = 1536
SLAB = 128 + WARM + T
TAU = 512
NTAU = (WARM + T) // TAU
NWT = WARM // TAU
SCT = 12
FFP = 2816
FFT = FFP // 128
MH = 4096
NEXP = 8
PASS_T = 1024
NPASS = T // PASS_T

LN32 = float(np.log(32.0))
ENGS = ("sync", "gpsimd", "tensor", "scalar", "vector")

TAPS = False
import os
SKIP_MOE = bool(os.environ.get("SKIP_MOE"))
SKIP_SPHASE = bool(os.environ.get("SKIP_SPHASE"))
JUNK_GROUP = not bool(os.environ.get("NO_JUNK_GROUP"))


class Sem:
    __slots__ = ("h", "v")

    def __init__(self, h):
        self.h = h
        self.v = 0


class Sched:
    def __init__(self, nc, stack):
        self.nc = nc
        self.stack = stack
        self.ops = {k: [] for k in ENGS}

    def sem(self, name):
        return Sem(self.stack.enter_context(self.nc.semaphore(name)))

    def emit(self, eng, fn, waits=(), incs=(), raw_incs=()):
        self.ops[eng].append((fn, tuple(waits), tuple(incs) + tuple(raw_incs)))
        for s, k in incs:
            s.v += k

    def set_pending(self, sem):
        assert getattr(self, "pending", None) is None
        sem.v += 1
        self.pending = sem

    def pop_pending(self):
        p = getattr(self, "pending", None)
        self.pending = None
        return p

    def flush_pending(self):
        p = self.pop_pending()
        if p is not None:
            self.tmark_raw(p)

    def flush(self):
        ops = self.ops
        self.ops = {k: [] for k in ENGS}

        def mk(eng):
            seq = ops[eng]

            def f(e):
                lastw = {}
                for fn, waits, incs in seq:
                    for (s, v) in waits:
                        if v > 0 and lastw.get(id(s), -1) < v:
                            e.wait_ge(s.h, v)
                            lastw[id(s)] = v
                    instr = fn(e)
                    if incs:
                        assert instr is not None, f"{eng} op returned None"
                        for s, k in incs:
                            instr = instr.then_inc(s.h, k)
            return f

        with self.nc.Block() as blk:
            for eng in ENGS:
                if ops[eng]:
                    getattr(blk, eng)(mk(eng))


class PSlot:
    def __init__(self, sched, ap, name):
        self.ap = ap
        self.prod = sched.sem(name + "_p")
        self.free = sched.sem(name + "_f")
        self.reads = 0


def mm_group(sch, slot, out_ap, mats, readers, extra_waits=(),
             defer_prod=False):
    prev = slot.reads
    n = len(mats)
    pend = sch.pop_pending()

    def tf(e, mats=mats, out_ap=out_ap, n=n, pend=pend):
        last = None
        lastw = {}
        for i, mat in enumerate(mats):
            if len(mat) == 3:
                for (s, v) in mat[2]:
                    if v > 0 and lastw.get(id(s), -1) < v:
                        e.wait_ge(s.h, v)
                        lastw[id(s)] = v
            last = e.matmul(out_ap, mat[0], mat[1],
                            start=(i == 0), stop=(i == n - 1))
            if i == 0 and pend is not None:
                last.then_inc(pend.h, 1)
        return last

    sch.emit("tensor", tf, waits=tuple(extra_waits) + ((slot.free, prev),))
    if defer_prod:
        sch.set_pending(slot.prod)
    else:
        sch.tmark(((slot.prod, 1),))
    pv = slot.prod.v
    for rd in readers:
        eng, fn = rd[0], rd[1]
        ew = rd[2] if len(rd) >= 3 else ()
        sch.emit(eng, fn, waits=((slot.prod, pv),) + tuple(ew),
                 incs=((slot.free, 1),))
        slot.reads += 1


def build_program(taps=False):
    nc = bass.Bass("TRN2", target_bir_lowering=False)

    xs_d = nc.dram_tensor("xs", [SLAB, D], F32, kind="ExternalInput").ap()
    pwT_d = nc.dram_tensor("pwT", [D, D], F32, kind="ExternalInput").ap()
    whg_d = nc.dram_tensor("whg", [D, 768], F32, kind="ExternalInput").ap()
    wout_d = nc.dram_tensor("wout", [1536, 256], F32, kind="ExternalInput").ap()
    dwd_d = nc.dram_tensor("dwdiag", [128, 4096], F32,
                           kind="ExternalInput").ap()
    dwb_d = nc.dram_tensor("dwb", [128, 8], F32, kind="ExternalInput").ap()
    pwb_d = nc.dram_tensor("pwb", [128, 8], F32, kind="ExternalInput").ap()
    gc_d = nc.dram_tensor("gc", [128, 8], F32, kind="ExternalInput").ap()
    gg_d = nc.dram_tensor("gg", [128, 8], F32, kind="ExternalInput").ap()
    gf_d = nc.dram_tensor("gf", [128, 8], F32, kind="ExternalInput").ap()
    wgate_d = nc.dram_tensor("wgate", [D, FFP], F32, kind="ExternalInput").ap()
    wval_d = nc.dram_tensor("wval", [D, FFP], F32, kind="ExternalInput").ap()
    wffo_d = nc.dram_tensor("wffo", [FFP, D], F32, kind="ExternalInput").ap()
    moewg_d = nc.dram_tensor("moewg", [128, 64], F32, kind="ExternalInput").ap()
    bg_d = nc.dram_tensor("bgx", [1, 8], F32, kind="ExternalInput").ap()
    b1c_d = nc.dram_tensor("b1c", [128, 256], F32, kind="ExternalInput").ap()
    b2r_d = nc.dram_tensor("b2r", [1, 8192], F32, kind="ExternalInput").ap()
    w1_d = nc.dram_tensor("w1", [NEXP, D, MH], F32, kind="ExternalInput").ap()
    w2_d = nc.dram_tensor("w2", [NEXP, MH, D], F32, kind="ExternalInput").ap()
    id_d = nc.dram_tensor("ident", [128, 128], F32, kind="ExternalInput").ap()
    zsc_d = nc.dram_tensor("zsc", [128, 1], F32, kind="ExternalInput").ap()
    out_d = nc.dram_tensor("out", [T, D], F32, kind="ExternalOutput").ap()

    s_dram = nc.dram_tensor("s_sp", [T, D], F32).ap()
    wgate_b = nc.dram_tensor("wgate_b", [D, FFP], BF16).ap()
    wval_b = nc.dram_tensor("wval_b", [D, FFP], BF16).ap()
    wffo_b = nc.dram_tensor("wffo_b", [FFP, D], BF16).ap()
    if taps:
        xn_dram = nc.dram_tensor("xn_sp", [CT, 128, T], BF16,
                                 kind="ExternalOutput").ap()
    else:
        xn_dram = nc.dram_tensor("xn_sp", [CT, 128, T], BF16).ap()
    tap_aps = {}
    if taps:
        tap_aps["tap_u"] = nc.dram_tensor("tap_u", [CT, 128, T], F32,
                                          kind="ExternalOutput").ap()
        tap_aps["tap_h"] = nc.dram_tensor("tap_h", [SCT, 128, T], F32,
                                          kind="ExternalOutput").ap()
        tap_aps["tap_s"] = nc.dram_tensor("tap_s", [T, D], F32,
                                          kind="ExternalOutput").ap()
        tap_aps["tap_g"] = nc.dram_tensor("tap_g", [128, NPASS * 8192], F32,
                                          kind="ExternalOutput").ap()
        tap_aps["tap_w"] = nc.dram_tensor("tap_w", [128, NPASS * 64], F32,
                                          kind="ExternalOutput").ap()
        tap_aps["tap_n2"] = nc.dram_tensor("tap_n2", [128, CT * 512], F32,
                                           kind="ExternalOutput").ap()
        tap_aps["tap_ff1"] = nc.dram_tensor("tap_ff1", [128, FFT * 512], F32,
                                            kind="ExternalOutput").ap()
        tap_aps["tap_h1"] = nc.dram_tensor("tap_h1", [128, 16 * PASS_T], F32,
                                           kind="ExternalOutput").ap()
        tap_aps["tap_y"] = nc.dram_tensor("tap_y", [128, NPASS * 8192], F32,
                                          kind="ExternalOutput").ap()

    with ExitStack() as big:
        sch = Sched(nc, big)
        sb = lambda name, shape, dt=F32: big.enter_context(
            nc.sbuf_tensor(name, shape, dt))

        ident = sb("identsb", [128, 128])
        onesf = sb("onesf", [1, 128])
        onesk = sb("onesk", [128, 1], BF16)
        ones_bf = sb("ones_bf", [1, 128], BF16)
        eps_ln = sb("eps_ln", [128, 1])
        eps_rms = sb("eps_rms", [1, 1])
        ln32t = sb("ln32t", [1, 1])
        zsc = sb("zsc_sb", [128, 1])

        pjunk = None if os.environ.get("NO_PJUNK") else big.enter_context(
            nc.psum_tensor("pjunk", [128, 512], F32))
        P = [PSlot(sch, big.enter_context(
            nc.psum_tensor(f"pmm{i}", [128, 512], F32)), f"pmm{i}")
            for i in range(4)]

        def tmark(incs, waits=()):
            pend = sch.pop_pending()
            if pend is not None:
                sch.tmark_raw(pend)

            def f(e):
                return e.matmul(pjunk[0:1, 0:1], ones_bf[0:1, 0:1],
                                ones_bf[0:1, 0:1], start=True, stop=True)
            sch.emit("tensor", f, waits=waits, incs=incs)

        def tmark_raw(sem):
            def f(e, sem=sem):
                i = e.matmul(pjunk[0:1, 0:1], ones_bf[0:1, 0:1],
                             ones_bf[0:1, 0:1], start=True, stop=True)
                return i.then_inc(sem.h, 1)
            sch.emit("tensor", f)
        sch.tmark = tmark
        sch.tmark_raw = tmark_raw
        sch.pending = None
        _pi = [0]

        def pnext():
            s = P[_pi[0] % 4]
            _pi[0] += 1
            return s

        ld = sch.sem("ld")

        def g_dma(out_ap, in_ap, waits=()):
            def f(e, o=out_ap, i=in_ap):
                return e.dma_start(out=o, in_=i)
            sch.emit("gpsimd", f, waits=waits, incs=((ld, 16),))
            return ld.v

        def g_memset(ap, val):
            def f(e, a=ap, v=val):
                return e.memset(a, v)
            sch.emit("gpsimd", f)

        tapq = sch.sem("tapq")

        def s_dma(out_ap, in_ap, waits=(), inc=None):
            if inc is None:
                inc = tapq
            def f(e, o=out_ap, i=in_ap):
                return e.dma_start(out=o, in_=i)
            sch.emit("sync", f, waits=waits, incs=((inc, 16),))

        with ExitStack() as mix:
            msb = lambda name, shape, dt=F32: mix.enter_context(
                nc.sbuf_tensor(name, shape, dt))

            pwT = msb("pwT_bf", [128, CT * D], BF16)
            whg = msb("whg_bf", [128, CT * 768], BF16)
            wout = msb("wout_bf", [128, SCT * 256], BF16)
            dwd_bf = msb("dwd_bf", [128, 4096], BF16)
            dwb = msb("dwb_sb", [128, 8])
            pwb = msb("pwb_sb", [128, 8])
            dwb_bf = msb("dwb_bf", [128, 8], BF16)
            gc1 = msb("gc1", [128, 8])
            gg1 = msb("gg1", [128, 8])
            gf1 = msb("gf1", [128, 8])
            bcomb = msb("bcomb", [128, 8])
            carry = msb("carry", [128, SCT])

            x_tm = [msb(f"x_tm{i}", [128, D]) for i in range(4)]
            lnt = [msb(f"lnt{i}", [128, 8]) for i in range(4)]
            lnsq = msb("lnsq", [128, 1024])
            act32 = msb("act32", [128, CT * 512])
            scrq = msb("scrq", [128, 2 * 512], BF16)
            c_ext = msb("c_ext", [128, CT * 515], BF16)
            bfA = msb("bfA", [128, CT * 512], BF16)
            bfN = msb("bfN", [128, CT * 512], BF16)
            rrow = msb("rrow", [1, 512])
            h32 = msb("h32", [128, 2 * 512])
            h_bf = msb("h_bf", [128, SCT * 512], BF16)
            a32 = msb("a32", [128, 2 * 512])
            z32 = msb("z32", [128, 2 * 512])
            sg32 = msb("sg32", [128, 2 * 512])
            gh32 = msb("gh32", [128, 2 * 512])
            ff1 = msb("ff1_bf", [128, FFT * 512], BF16)
            sil = msb("sil", [128, 2 * 512])
            v_tm = msb("v_tm", [128, 4 * D])
            s_sp = [msb(f"s_sp{i}", [128, 512]) for i in range(2)]
            xn_stg = [msb(f"xn_stg{i}", [128, 512], BF16) for i in range(2)]
            wgs = msb("wgs", [128, 2 * CT * 128], BF16)
            wvs = msb("wvs", [128, 2 * CT * 128], BF16)
            wos = msb("wos", [128, 4 * D], BF16)

            P_tr = PSlot(sch, mix.enter_context(
                nc.psum_tensor("ptr", [128, 512], F32)), "ptr")
            P_row = PSlot(sch, mix.enter_context(
                nc.psum_tensor("prow", [1, 512], F32)), "prow")
            P_bc = PSlot(sch, mix.enter_context(
                nc.psum_tensor("pbc", [128, 512], F32)), "pbc")

            g_dma(pwT[:].rearrange("p (k e) -> p k e", e=D),
                  pwT_d.rearrange("(k p) e -> p k e", p=128))
            g_dma(whg[:].rearrange("p (k e) -> p k e", e=768),
                  whg_d.rearrange("(k p) e -> p k e", p=128))
            g_dma(wout[:].rearrange("p (k e) -> p k e", e=256),
                  wout_d.rearrange("(k p) e -> p k e", p=128))
            g_dma(dwd_bf[:], dwd_d)
            g_dma(dwb[:], dwb_d)
            g_dma(pwb[:], pwb_d)
            g_dma(gc1[:], gc_d)
            g_dma(gg1[:], gg_d)
            g_dma(gf1[:], gf_d)
            g_dma(ident[:], id_d)
            g_dma(zsc[:], zsc_d)
            ldtot = ld.v
            g_memset(onesf[:], 1.0)
            g_memset(onesk[:], 1.0)
            g_memset(ones_bf[:], 1.0)
            g_memset(eps_ln[:], 1e-5)
            g_memset(eps_rms[:], 1e-18)
            g_memset(ln32t[:], LN32)
            g_memset(carry[:], 0.0)
            init_gp = sch.sem("init_gp")
            sch.emit("gpsimd", lambda e: e.engine_nop(),
                     waits=((ld, ldtot),), incs=((init_gp, 1),))

            gam_done = sch.sem("gam_done")
            for t_ in (gc1, gg1, gf1):
                def f(e, t_=t_):
                    return e.add(t_[:], t_[:], 1.0)
                sch.emit("scalar", f, waits=((init_gp, 1),),
                         incs=((gam_done, 1),))

            vcd = sch.sem("vcd")
            sch.emit("vector", lambda e: e.tensor_copy(dwb_bf[:], dwb[:]),
                     waits=((init_gp, 1),), incs=((vcd, 1),))

            for m in range(CT):
                slot = pnext()
                mats = [(pwT[:, k * D + m * 128:k * D + (m + 1) * 128],
                         dwb_bf[:, k:k + 1]) for k in range(CT)]

                def rd(e, m=m, s=slot):
                    return e.scalar_tensor_tensor(
                        bcomb[:, m:m + 1], s.ap[:, 0:1], 1.0,
                        pwb[:, m:m + 1], op0=ALU.mult, op1=ALU.add)
                mm_group(sch, slot, slot.ap[:, 0:1], mats, [("vector", rd)],
                         extra_waits=((vcd, 1), (init_gp, 1)))

            sch.flush()

            gate_ov, val_ov, wo_ov = {}, {}, {}
            for c in range(FFT):
                prs = c % 2
                w_in = ((ld, gate_ov[c - 2]),) if c >= 2 else ()
                iv = g_dma(wgs[:, prs * CT * 128:(prs + 1) * CT * 128]
                           .rearrange("p (k m) -> p k m", m=128),
                           bass.AP(wgate_d.tensor, c * 128,
                                   [[FFP, 128], [128 * FFP, CT], [1, 128]]),
                           waits=w_in)
                gate_ov[c] = g_dma(
                    bass.AP(wgate_b.tensor, c * 128,
                            [[FFP, 128], [128 * FFP, CT], [1, 128]]),
                    wgs[:, prs * CT * 128:(prs + 1) * CT * 128]
                    .rearrange("p (k m) -> p k m", m=128),
                    waits=((ld, iv),))
                w_in = ((ld, val_ov[c - 2]),) if c >= 2 else ()
                iv = g_dma(wvs[:, prs * CT * 128:(prs + 1) * CT * 128]
                           .rearrange("p (k m) -> p k m", m=128),
                           bass.AP(wval_d.tensor, c * 128,
                                   [[FFP, 128], [128 * FFP, CT], [1, 128]]),
                           waits=w_in)
                val_ov[c] = g_dma(
                    bass.AP(wval_b.tensor, c * 128,
                            [[FFP, 128], [128 * FFP, CT], [1, 128]]),
                    wvs[:, prs * CT * 128:(prs + 1) * CT * 128]
                    .rearrange("p (k m) -> p k m", m=128),
                    waits=((ld, iv),))
            for k in range(FFT):
                prs = k % 4
                w_in = ((ld, wo_ov[k - 4]),) if k >= 4 else ()
                iv = g_dma(wos[:, prs * D:(prs + 1) * D],
                           wffo_d[k * 128:(k + 1) * 128, :], waits=w_in)
                wo_ov[k] = g_dma(wffo_b[k * 128:(k + 1) * 128, :],
                                 wos[:, prs * D:(prs + 1) * D],
                                 waits=((ld, iv),))
            pc_all = wo_ov[FFT - 1]

            xq = [sch.sem(f"xq{i}") for i in range(4)]
            xfree = sch.sem("xfree")
            ln_vs = sch.sem("ln_vs")
            ln_sv = sch.sem("ln_sv")
            xn_rdy = sch.sem("xn_rdy")
            sq_done = sch.sem("sq_done")
            stats_mm = sch.sem("stats_mm")
            rr_done = sch.sem("rr_done")
            ap_done = sch.sem("ap_done")
            y1_done = sch.sem("y1_done")
            u_done = sch.sem("u_done")
            sc_s = sch.sem("sc_s")
            scan_done = sch.sem("scan_done")
            hbf_done = sch.sem("hbf_done")
            v_done = sch.sem("v_done")
            vtm_done = sch.sem("vtm_done")
            wg_used = sch.sem("wg_used")
            wo_used = sch.sem("wo_used")
            sil_rdy = sch.sem("sil_rdy")
            ff1_rdy = sch.sem("ff1_rdy")
            smm_done = sch.sem("smm_done")
            s_ev = sch.sem("s_ev")
            s_st = sch.sem("s_st")
            xn_st = sch.sem("xn_st")
            pw_done = sch.sem("pw_done")
            hg_done = sch.sem("hg_done")
            xn_cp = sch.sem("xn_cp")
            vch = sch.sem("vch")

            h32hist = {}
            ctr = dict(jglob=0, gsc=0, rms=0, wgvchunk=0, wochunk=0,
                       tau_real=0, lnv={}, xnspill=0, sev=0)

            def emit_ln(jg, row0):
                js = jg % 4
                xt, lt = x_tm[js], lnt[js]
                wv = ((xfree, jg - 3),) if jg >= 4 else ()
                s_dma(xt[:], xs_d[row0:row0 + 128, :], waits=wv, inc=xq[js])
                xv = xq[js].v
                lnwar = ((ln_vs, ctr["lnv"].get(jg - 4, 0)),) if jg >= 4 else ()

                def v_red(e, xt=xt, lt=lt):
                    return e.reduce_sum(lt[:, 0:1], xt[:], axis=AXX)
                sch.emit("vector", v_red, waits=((xq[js], xv),),
                         incs=((ln_vs, 1),))
                rv = ln_vs.v

                def s_m(e, lt=lt):
                    return e.mul(lt[:, 1:2], lt[:, 0:1], 1.0 / D)
                sch.emit("scalar", s_m, waits=((ln_vs, rv),) + lnwar,
                         incs=((ln_sv, 1),))

                def s_sq(e, xt=xt, lt=lt):
                    return e.activation(lnsq[:], xt[:], AF.Square,
                                        accum_out=lt[:, 2:3])
                sch.emit("scalar", s_sq, waits=((xq[js], xv),),
                         incs=((ln_sv, 1),))

                def s_msq(e, lt=lt):
                    return e.square(lt[:, 3:4], lt[:, 1:2])
                sch.emit("scalar", s_msq, incs=((ln_sv, 1),))
                mv = ln_sv.v

                def v_var2(e, lt=lt):
                    return e.scalar_tensor_tensor(
                        lt[:, 4:5], lt[:, 2:3], 1.0 / D, lt[:, 3:4],
                        op0=ALU.mult, op1=ALU.subtract)
                sch.emit("vector", v_var2, waits=((ln_sv, mv),),
                         incs=((ln_vs, 1),))
                vv = ln_vs.v

                def s_ln(e, lt=lt):
                    return e.activation(lt[:, 5:6], lt[:, 4:5], AF.Ln,
                                        bias=eps_ln[:, 0:1])
                sch.emit("scalar", s_ln, waits=((ln_vs, vv),))

                def s_exp(e, lt=lt):
                    return e.activation(lt[:, 6:7], lt[:, 5:6], AF.Exp,
                                        scale=-0.5)
                sch.emit("scalar", s_exp, incs=((ln_sv, 1),))
                ev = ln_sv.v

                def v_ap(e, xt=xt, lt=lt):
                    return e.tensor_scalar(xt[:], xt[:], lt[:, 1:2], lt[:, 6:7],
                                           op0=ALU.subtract, op1=ALU.mult)
                sch.emit("vector", v_ap, waits=((ln_sv, ev),),
                         incs=((ln_vs, 1),))
                ctr["lnv"][jg] = ln_vs.v
                return ln_vs.v

            def emit_transpose_j(jg, dstcol, lnv):
                js = jg % 4
                xt = x_tm[js]
                for grp in range(2):
                    slot = P_tr
                    prev = slot.reads

                    def tf(e, xt=xt, grp=grp, s=slot):
                        last = None
                        for cc in range(4):
                            c = grp * 4 + cc
                            last = e.transpose(
                                s.ap[:, cc * 128:(cc + 1) * 128],
                                xt[:, c * 128:(c + 1) * 128], ident[:])
                        return last
                    tw = [(ln_vs, lnv), (slot.free, prev)]
                    sch.emit("tensor", tf, waits=tuple(tw))
                    sch.tmark(((slot.prod, 1),))
                    pv = slot.prod.v

                    def s_ev_f(e, grp=grp, s=slot, dstcol=dstcol):
                        dst = bass.AP(act32, grp * 4 * 512 + dstcol,
                                      [[CT * 512, 128], [512, 4], [1, 128]])
                        return e.copy(dst, s.ap[:].rearrange(
                            "p (c n) -> p c n", n=128))
                    sch.emit("scalar", s_ev_f, waits=((slot.prod, pv),),
                             incs=((slot.free, 1),))
                    slot.reads += 1
                sch.emit("tensor", lambda e: e.drain())
                sch.tmark(((xfree, 1),))
                return slot.free.v

            def emit_rms(gam, dst, dst_stride, dst_off, width, src_rdy,
                         sq_extra=(), ap_extra=()):
                call = ctr["rms"]
                ctr["rms"] += 1
                slot = P_row
                prev = slot.reads
                sqv = []
                for c in range(CT):
                    gsi = call * CT + c
                    war = ((stats_mm, gsi - 1),) if gsi >= 2 else ()

                    def s_sq(e, c=c, width=width, gsi=gsi):
                        return e.activation(
                            scrq[:, (gsi % 2) * 512:(gsi % 2) * 512 + width],
                            act32[:, c * 512:c * 512 + width], AF.Square)
                    sch.emit("scalar", s_sq,
                             waits=tuple(src_rdy) + tuple(sq_extra) + war,
                             incs=((sq_done, 1),))
                    sqv.append(sq_done.v)

                    def tmf(e, c=c, width=width, gsi=gsi, s=slot):
                        return e.matmul(
                            s.ap[0:1, 0:width], onesk[:],
                            scrq[:, (gsi % 2) * 512:(gsi % 2) * 512 + width],
                            start=(c == 0), stop=(c == CT - 1))
                    w = [(sq_done, sqv[-1])]
                    if c == 0:
                        w.append((slot.free, prev))
                    if c == CT - 1:
                        sch.emit("tensor", tmf, waits=tuple(w))
                        sch.tmark(((slot.prod, 1),))
                        sch.tmark(((stats_mm, 1),))
                    else:
                        sch.emit("tensor", tmf, waits=tuple(w),
                                 incs=((stats_mm, 1),))
                pv = slot.prod.v

                def s_lnr(e, width=width, s=slot):
                    return e.activation(rrow[0:1, 0:width], s.ap[0:1, 0:width],
                                        AF.Ln, bias=eps_rms[0:1, 0:1])
                sch.emit("scalar", s_lnr, waits=((slot.prod, pv),),
                         incs=((slot.free, 1),))
                slot.reads += 1

                def s_expr(e, width=width):
                    return e.activation(rrow[0:1, 0:width], rrow[0:1, 0:width],
                                        AF.Exp, scale=-0.5, bias=ln32t[0:1, 0:1])
                sch.emit("scalar", s_expr, incs=((rr_done, 1),))
                rrv = rr_done.v

                bslot = P_bc
                prevb = bslot.reads

                def tbf(e, width=width, s=bslot):
                    return e.matmul(s.ap[:, 0:width], onesf[:],
                                    rrow[0:1, 0:width], start=True, stop=True)
                sch.emit("tensor", tbf,
                         waits=((rr_done, rrv), (bslot.free, prevb)))
                sch.tmark(((bslot.prod, 1),))
                pbv = bslot.prod.v
                for c in range(CT):
                    def v_ap(e, c=c, dst=dst, gam=gam, width=width, s=bslot,
                             dst_stride=dst_stride, dst_off=dst_off):
                        return e.scalar_tensor_tensor(
                            dst[:, c * dst_stride + dst_off:
                                c * dst_stride + dst_off + width],
                            act32[:, c * 512:c * 512 + width],
                            gam[:, c:c + 1], s.ap[:, 0:width],
                            op0=ALU.mult, op1=ALU.mult)
                    incs = []
                    if c == CT - 1:
                        incs = [(bslot.free, 1)]
                    sch.emit("vector", v_ap,
                             waits=((bslot.prod, pbv),) + tuple(ap_extra),
                             incs=tuple(incs))
                bslot.reads += 1
                return sqv, bslot.free.v

            lnv = emit_ln(0, 0)
            trv = emit_transpose_j(0, 0, lnv)
            emit_rms(gc1, c_ext, 515, 387, 128, ((P_tr.free, trv),),
                     ap_extra=((gam_done, 3),))

            front_out = {}
            conv_prev = [()]

            ln_out = {}

            def emit_front_ln(tau):
                base = 128 + tau * TAU
                lns = []
                for j in range(4):
                    ctr["jglob"] += 1
                    jg = ctr["jglob"]
                    lns.append((jg, emit_ln(jg, base + j * 128)))
                ln_out[tau] = lns

            def emit_front_rest(tau):
                real = tau >= NWT
                for j, (jg, lnv) in enumerate(ln_out[tau]):
                    xrv = emit_transpose_j(jg, j * 128, lnv)

                if real:
                    for c in range(CT):
                        idx = ctr["xnspill"]
                        ctr["xnspill"] += 1
                        sl = idx % 2
                        war = ((xn_st, 16 * (idx - 1)),) if idx >= 2 else ()

                        def v_cp(e, c=c, sl=sl):
                            return e.tensor_copy(
                                xn_stg[sl][:],
                                act32[:, c * 512:(c + 1) * 512])
                        sch.emit("vector", v_cp,
                                 waits=((P_tr.free, xrv),) + war,
                                 incs=((xn_cp, 1),))
                        s_dma(xn_dram[c, :, (tau - NWT) * 512:(tau - NWT + 1) * 512],
                              xn_stg[sl][:], waits=((xn_cp, xn_cp.v),),
                              inc=xn_st)

                for c in range(CT):
                    def v_halo(e, c=c):
                        return e.tensor_copy(c_ext[:, c * 515:c * 515 + 3],
                                             c_ext[:, c * 515 + 512:c * 515 + 515])
                    hw = conv_prev[0] if c == 0 else ()
                    sch.emit("vector", v_halo, waits=hw)
                sqc, apc = emit_rms(gc1, c_ext, 515, 3, 512,
                                    ((P_tr.free, xrv),))

                front_out[tau] = (sqc, apc)

            def emit_conv(tau):
                sqc, apc = front_out[tau]
                hgv0 = hg_done.v
                for c in range(CT):
                    slot = pnext()
                    mats = [(dwd_bf[:, (c * 4 + k) * 128:(c * 4 + k + 1) * 128],
                             c_ext[:, c * 515 + k:c * 515 + k + 512])
                            for k in range(4)]

                    def cv_ev(e, c=c, s=slot):
                        return e.copy(bfA[:, c * 512:(c + 1) * 512], s.ap[:])
                    mm_group(sch, slot, slot.ap[:], mats,
                             [("scalar", cv_ev,
                               ((hg_done, hgv0),) if c == 0 else ())],
                             extra_waits=((P_bc.free, apc),) if c == 0 else (),
                             defer_prod=(c < CT - 1))
                cw = tuple((pp.free, pp.free.v) for pp in P)
                conv_prev[0] = cw
                return cw

            pending_s = [None]

            def emit_rest(tau, hoist_ln, hoist_rest):
                real = tau >= NWT
                sqc, apc = front_out[tau]
                box = {}

                def emit_convpw():
                    cw = emit_conv(tau)
                    for m in range(CT):
                        slot = pnext()
                        mats = [(pwT[:, k * D + m * 128:k * D + (m + 1) * 128],
                                 bfA[:, k * 512:(k + 1) * 512])
                                for k in range(CT)]

                        def u_rd(e, m=m, s=slot):
                            return e.scalar_tensor_tensor(
                                act32[:, m * 512:(m + 1) * 512], s.ap[:],
                                bcomb[:, m:m + 1],
                                act32[:, m * 512:(m + 1) * 512],
                                op0=ALU.add, op1=ALU.add)
                        mm_group(sch, slot, slot.ap[:], mats,
                                 [("vector", u_rd, ((sq_done, sqc[m]),))],
                                 extra_waits=cw if m == 0 else (),
                                 defer_prod=(m < CT - 1))
                    sch.tmark(((pw_done, 1),))
                    box["pwv"] = pw_done.v
                    box["uw"] = tuple((pp.free, pp.free.v) for pp in P)
                if pending_s[0] is not None:
                    pending_s[0](emit_convpw)
                    pending_s[0] = None
                else:
                    emit_convpw()
                pwv = box["pwv"]
                uw = box["uw"]
                if taps and real:
                    for c in range(CT):
                        s_dma(tap_aps["tap_u"][c, :, (tau - NWT) * 512:(tau - NWT + 1) * 512],
                              act32[:, c * 512:(c + 1) * 512],
                              waits=uw)

                sqg, apg = emit_rms(gg1, bfA, 512, 0, 512, uw,
                                    sq_extra=((pw_done, pwv),),
                                    ap_extra=((pw_done, pwv),))
                if not real:
                    hoist_ln()

                def emit_wout_head(h):
                    hbv = hbf_done.v
                    for m in (2 * h, 2 * h + 1):
                        slot = pnext()
                        mats = [(wout[:, (3 * h + kk) * 256 + (m % 2) * 128:
                                     (3 * h + kk) * 256 + (m % 2 + 1) * 128],
                                 h_bf[:, (3 * h + kk) * 512:(3 * h + kk + 1) * 512])
                                for kk in range(3)]

                        def v_rd(e, m=m, s=slot):
                            return e.tensor_tensor(
                                act32[:, m * 512:(m + 1) * 512], s.ap[:],
                                act32[:, m * 512:(m + 1) * 512], op=ALU.add)
                        mm_group(sch, slot, slot.ap[:], mats,
                                 [("vector", v_rd, ((sq_done, sqg[m]),))],
                                 extra_waits=((hbf_done, hbv),),
                                 defer_prod=(m < 2 * h + 1 or h < 3))

                for h in range(4):
                    for i in range(3):
                        gsc = ctr["gsc"]
                        ctr["gsc"] += 1
                        so = (gsc % 2) * 512
                        war = ((scan_done, gsc - 1),) if gsc >= 2 else ()
                        hid = pnext()
                        gat = pnext()
                        hmats = [(whg[:, (2 * h + kk) * 768 + i * 128:
                                      (2 * h + kk) * 768 + (i + 1) * 128],
                                  bfA[:, (2 * h + kk) * 512:(2 * h + kk + 1) * 512])
                                 for kk in range(2)]
                        gmats = [(whg[:, (2 * h + kk) * 768 + (3 + i) * 128:
                                      (2 * h + kk) * 768 + (4 + i) * 128],
                                  bfA[:, (2 * h + kk) * 512:(2 * h + kk + 1) * 512])
                                 for kk in range(2)]

                        def r_sg(e, s=hid, so=so):
                            return e.activation(sg32[:, so:so + 512], s.ap[:],
                                                AF.Sigmoid)

                        def r_mk(e, s=hid, so=so):
                            return e.scalar_tensor_tensor(
                                gh32[:, so:so + 512], s.ap[:], 0.5,
                                sg32[:, so:so + 512], op0=ALU.add, op1=ALU.max)

                        def r_a(e, s=gat, so=so):
                            return e.activation(a32[:, so:so + 512], s.ap[:],
                                                AF.Sigmoid, scale=-1.0)

                        def r_z(e, s=gat, so=so):
                            return e.activation(z32[:, so:so + 512], s.ap[:],
                                                AF.Sigmoid)
                        mm_group(sch, hid, hid.ap[:], hmats,
                                 [("scalar", r_sg, war),
                                  ("vector", r_mk,
                                   war + ((hid.free, hid.free.v + 1),))],
                                 extra_waits=((P_bc.free, apg),),
                                 defer_prod=True)
                        hfv = hid.free.v
                        mm_group(sch, gat, gat.ap[:], gmats,
                                 [("scalar", r_a), ("scalar", r_z)],
                                 defer_prod=(h * 3 + i < 11))
                        gfv = gat.free.v

                        def v_b(e, so=so):
                            return e.tensor_tensor(gh32[:, so:so + 512],
                                                   z32[:, so:so + 512],
                                                   gh32[:, so:so + 512],
                                                   op=ALU.mult)
                        sch.emit("vector", v_b, waits=((gat.free, gfv),))
                        sc = h * 3 + i

                        def v_scan(e, so=so, sc=sc):
                            return e.tensor_tensor_scan(
                                h32[:, 0:512], a32[:, so:so + 512],
                                gh32[:, so:so + 512],
                                carry[:, sc:sc + 1], op0=ALU.mult, op1=ALU.add)
                        sch.emit("vector", v_scan, incs=((vch, 1),))
                        vchv = vch.v

                        def v_cr(e, sc=sc):
                            return e.tensor_copy(carry[:, sc:sc + 1],
                                                 h32[:, 511:512])
                        sch.emit("vector", v_cr, waits=((vch, vchv),),
                                 incs=((scan_done, 1),))
                        if real:
                            def v_hbf(e, sc=sc):
                                return e.tensor_copy(
                                    h_bf[:, sc * 512:(sc + 1) * 512],
                                    h32[:, 0:512])
                            sch.emit("vector", v_hbf, incs=((hbf_done, 1),))
                sch.tmark(((hg_done, 1),))
                hgv = hg_done.v
                if taps and real:
                    for sc in range(SCT):
                        g_dma(tap_aps["tap_h"][sc, :, (tau - NWT) * 512:(tau - NWT + 1) * 512],
                              h_bf[:, sc * 512:(sc + 1) * 512],
                              waits=((hbf_done, hbf_done.v),))

                if tau == NWT - 1:
                    def v_zsc(e):
                        return e.tensor_scalar(carry[:], carry[:],
                                               zsc[:, 0:1], None, op0=ALU.mult)
                    sch.emit("vector", v_zsc)

                if not real:
                    return

                for h in range(4):
                    emit_wout_head(h)
                vw = tuple((pp.free, pp.free.v) for pp in P)

                sqf, apf = emit_rms(gf1, bfN, 512, 0, 512, vw,
                                    sq_extra=((hg_done, hgv),),
                                    ap_extra=((hg_done, hgv),))

                if taps and tau == NWT:
                    for c in range(CT):
                        g_dma(tap_aps["tap_n2"][:, c * 512:(c + 1) * 512],
                              bfA[:, c * 512:(c + 1) * 512],
                              waits=((P_bc.free, apf),))

                for tb in range(4):
                    for grp in range(2):
                        slot = P_tr
                        prev = slot.reads

                        def tf(e, tb=tb, grp=grp, s=slot):
                            last = None
                            for cc in range(4):
                                c = grp * 4 + cc
                                last = e.transpose(
                                    s.ap[:, cc * 128:(cc + 1) * 128],
                                    act32[:, c * 512 + tb * 128:
                                          c * 512 + (tb + 1) * 128], ident[:])
                            return last
                        sch.emit("tensor", tf,
                                 waits=vw + ((slot.free, prev),))
                        sch.tmark(((slot.prod, 1),))
                        pv = slot.prod.v

                        def sv_ev(e, tb=tb, grp=grp, s=slot):
                            dst = v_tm[:, tb * D + grp * 4 * 128:
                                       tb * D + (grp + 1) * 4 * 128]
                            return e.copy(dst, s.ap[:, 0:512])
                        sch.emit("scalar", sv_ev, waits=((slot.prod, pv),),
                                 incs=((slot.free, 1),))
                        slot.reads += 1
                vtmv = P_tr.free.v
                sch.emit("tensor", lambda e: e.drain())
                hoist_ln()

                rt = ctr["tau_real"]
                ctr["tau_real"] += 1
                ffmarks = {}
                for mfi in range(FFT):
                    mf = mfi
                    ck = ctr["wgvchunk"]
                    ctr["wgvchunk"] += 1
                    rs = ck % 2
                    wwar = ((wg_used, ck - 1),) if ck >= 2 else ()
                    g_dma(wgs[:, rs * CT * 128:(rs + 1) * CT * 128]
                          .rearrange("p (k m) -> p k m", m=128),
                          bass.AP(wgate_b.tensor, mf * 128,
                                  [[FFP, 128], [128 * FFP, CT], [1, 128]]),
                          waits=wwar + (((ld, pc_all),) if ck < 2 else ()))
                    ckv = g_dma(wvs[:, rs * CT * 128:(rs + 1) * CT * 128]
                                .rearrange("p (k m) -> p k m", m=128),
                                bass.AP(wval_b.tensor, mf * 128,
                                        [[FFP, 128], [128 * FFP, CT], [1, 128]]),
                                waits=())
                    slg = pnext()
                    slv = pnext()
                    gmats = [(wgs[:, rs * CT * 128 + k * 128:
                                  rs * CT * 128 + (k + 1) * 128],
                              bfN[:, k * 512:(k + 1) * 512]) for k in range(CT)]
                    vmats = [(wvs[:, rs * CT * 128 + k * 128:
                                  rs * CT * 128 + (k + 1) * 128],
                              bfN[:, k * 512:(k + 1) * 512]) for k in range(CT)]

                    def r_sil(e, mf=mf, s=slg):
                        return e.activation(
                            sil[:, (mf % 2) * 512:(mf % 2) * 512 + 512],
                            s.ap[:], AF.Silu)
                    sil_war = (ffmarks[mf - 2],) if mfi >= 2 else ()
                    mm_group(sch, slg, slg.ap[:], gmats,
                             [("scalar", r_sil, sil_war)],
                             extra_waits=((ld, ckv), (P_bc.free, apf)),
                             defer_prod=True)
                    silv = (slg.free, slg.free.v)

                    def r_ff1(e, mf=mf, s=slv):
                        return e.tensor_tensor(
                            ff1[:, mf * 512:(mf + 1) * 512],
                            sil[:, (mf % 2) * 512:(mf % 2) * 512 + 512],
                            s.ap[:], op=ALU.mult)
                    ffw_w = [silv]
                    if rt > 0 and mf == 0:
                        ffw_w.append((smm_done, rt))
                    mm_group(sch, slv, slv.ap[:], vmats,
                             [("vector", r_ff1, tuple(ffw_w))],
                             defer_prod=True)
                    ffmarks[mf] = (slv.free, slv.free.v)
                    sch.tmark(((wg_used, 1),))
                ffw_pairs = {}
                for sm, vv_ in ffmarks.values():
                    ffw_pairs[id(sm)] = (sm, vv_)
                ffw_pairs = tuple(ffw_pairs.values())

                if taps and tau == NWT:
                    for mf in range(FFT):
                        g_dma(tap_aps["tap_ff1"][:, mf * 512:(mf + 1) * 512],
                              ff1[:, mf * 512:(mf + 1) * 512],
                              waits=ffw_pairs)

                hoist_rest()

                def emit_sphase(mid_cb, tau=tau, ffw_pairs=ffw_pairs,
                                vtmv=vtmv):
                  if SKIP_SPHASE:
                      if mid_cb is not None:
                          mid_cb()
                      return
                  for dh in range(2):
                      if dh == 1 and mid_cb is not None:
                          mid_cb()
                      chunk_tgt = []
                      for k in range(FFT):
                          ok = ctr["wochunk"]
                          ctr["wochunk"] += 1
                          owar = ((wo_used, ok - 3),) if ok >= 4 else ()
                          tgt = g_dma(wos[:, (ok % 4) * D:(ok % 4 + 1) * D],
                                      wffo_b[k * 128:(k + 1) * 128, :],
                                      waits=owar +
                                      (((ld, pc_all),) if ok < 4 else ()))
                          chunk_tgt.append((ok % 4, tgt))
                      for k in range(FFT):
                          rs, tgt = chunk_tgt[k]
                          for tb in range(4):
                              slot = P[tb]
                              w = [(ld, tgt)]
                              if k == 0:
                                  w.append((slot.free, slot.reads))
                                  if dh == 0:
                                      w.extend(ffw_pairs)
                              incs = []
                              if tb == 3 and k < FFT - 1:
                                  incs.append((wo_used, 1))
                              if not JUNK_GROUP and k == FFT - 1:
                                  incs.append((slot.prod, 1))

                              def tmm(e, k=k, tb=tb, rs=rs, dh=dh, s=slot):
                                  return e.matmul(
                                      s.ap[:],
                                      ff1[:, k * 512 + tb * 128:
                                          k * 512 + (tb + 1) * 128],
                                      wos[:, rs * D + dh * 512:
                                          rs * D + (dh + 1) * 512],
                                      start=(k == 0), stop=(k == FFT - 1))
                              sch.emit("tensor", tmm, waits=tuple(w),
                                       incs=tuple(incs))
                      sch.tmark(((wo_used, 1),))
                      if JUNK_GROUP:
                          for tb in range(4):
                              sch.tmark(((P[tb].prod, 1),))
                      for tb in range(4):
                          slot = P[tb]
                          pv = slot.prod.v
                          sidx = ctr["sev"]
                          ctr["sev"] += 1
                          ssl = sidx % 2
                          swar = ((s_st, 16 * (sidx - 1)),) if sidx >= 2 else ()

                          def s_rd(e, tb=tb, dh=dh, ssl=ssl, s=slot):
                              return e.tensor_tensor(
                                  s_sp[ssl][:, 0:512], s.ap[:],
                                  v_tm[:, tb * D + dh * 512:
                                       tb * D + (dh + 1) * 512],
                                  op=ALU.add)
                          sch.emit("vector", s_rd,
                                   waits=((slot.prod, pv), (P_tr.free, vtmv))
                                   + swar,
                                   incs=((slot.free, 1),))
                          slot.reads += 1
                          sev = (slot.free, slot.free.v)
                          r0 = (tau - NWT) * 512 + tb * 128
                          dst = bass.AP(s_dram.tensor, r0 * D + dh * 512,
                                        [[D, 128], [1, 512]])
                          s_dma(dst, s_sp[ssl][:, 0:512],
                                waits=(sev,), inc=s_st)
                          if taps:
                              tdst = bass.AP(tap_aps["tap_s"].tensor,
                                             r0 * D + dh * 512,
                                             [[D, 128], [1, 512]])
                              s_dma(tdst, s_sp[ssl][:, 0:512],
                                    waits=(sev,))
                  sch.tmark(((smm_done, 1),))
                pending_s[0] = emit_sphase

            emit_front_ln(0)
            emit_front_rest(0)
            for tau in range(NTAU):
                done = {"ln": False, "rest": False}

                def _cb_ln(tau=tau, done=done):
                    if tau + 1 < NTAU and not done["ln"]:
                        emit_front_ln(tau + 1)
                        done["ln"] = True

                def _cb_rest(tau=tau, done=done):
                    _cb_ln(tau, done)
                    if tau + 1 < NTAU and not done["rest"]:
                        emit_front_rest(tau + 1)
                        done["rest"] = True
                emit_rest(tau, _cb_ln, _cb_rest)
                _cb_ln(tau, done)
                _cb_rest(tau, done)
            if pending_s[0] is not None:
                pending_s[0](None)
                pending_s[0] = None

            sch.flush()

        if SKIP_MOE:
            def fin_dummy(e):
                return e.nop()
            sch.emit("sync", fin_dummy, waits=((s_st, s_st.v),))
            sch.flush()
            return nc
        with ExitStack() as moe:
            qsb = lambda name, shape, dt=F32: moe.enter_context(
                nc.sbuf_tensor(name, shape, dt))
            h1h = qsb("h1h", [128, 16 * PASS_T], BF16)
            w1s = qsb("w1s", [128, 2 * CT * 512], BF16)
            w2h = qsb("w2h", [128, 16 * D], BF16)
            ystg = qsb("ystg", [128, 8 * D])
            g_acc = qsb("g_acc", [128, 8 * D])
            xnm2 = [qsb(f"xnm{i}", [128, CT * PASS_T], BF16)
                    for i in range(2)]
            w_all = qsb("w_all", [128, 64])
            wgsb = qsb("wgsb", [128, 64], BF16)
            bg_bc = qsb("bg_bc", [128, 8])
            b1c = qsb("b1c_sb", [128, 256])
            b2s = qsb("b2s_sb", [1, 2048], BF16)
            lgt = qsb("lgt", [128, 2 * 8])
            elg = qsb("elg", [128, 2 * 8])
            pst = qsb("pst", [128, 2 * 8])
            s_rl = [qsb(f"s_rl{i}", [128, D]) for i in range(2)]
            x0_t = [qsb(f"x0_t{i}", [128, D]) for i in range(2)]
            o_sb = [qsb(f"o_sb{i}", [128, D]) for i in range(2)]

            P_lg = PSlot(sch, moe.enter_context(
                nc.psum_tensor("plg", [128, 512], F32)), "plg")
            xnm_q = sch.sem("xnm_q")
            w1_used = sch.sem("w1_used")
            w2_used = sch.sem("w2_used")
            h1_rdy = sch.sem("h1_rdy")
            ystA = sch.sem("ystA")
            gacc_done = sch.sem("gacc_done")
            moe_vs = sch.sem("moe_vs")
            moe_sv = sch.sem("moe_sv")
            pre_done = sch.sem("pre_done")
            gsil = sch.sem("gsil")
            srl_q = sch.sem("srl_q")
            x0_q = sch.sem("x0_q")
            fin_v = sch.sem("fin_v")
            out_q = sch.sem("out_q")
            mch = sch.sem("mch")

            g_dma(wgsb[:], moewg_d)
            g_dma(b1c[:], b1c_d)
            bgv = g_dma(bg_bc[0:1, 0:8], bg_d)
            slot = pnext()

            def bgf(e, s=slot):
                return e.matmul(s.ap[:, 0:8], onesf[:], bg_bc[0:1, 0:8],
                                start=True, stop=True)
            sch.emit("tensor", bgf,
                     waits=((ld, bgv), (slot.free, slot.reads)))
            sch.tmark(((slot.prod, 1),))

            def bgr(e, s=slot):
                return e.tensor_copy(bg_bc[:], s.ap[:, 0:8])
            sch.emit("vector", bgr, waits=((slot.prod, slot.prod.v),),
                     incs=((slot.free, 1),))
            slot.reads += 1

            xnvs = []
            for p in range(NPASS):
                s_dma(xnm2[p % 2][:].rearrange("p (c t) -> p c t", t=PASS_T),
                      bass.AP(xn_dram.tensor, p * PASS_T,
                              [[T, 128], [128 * T, CT], [1, PASS_T]]),
                      waits=(), inc=xnm_q)
                xnvs.append(xnm_q.v)

            for p in range(NPASS):
                xnm = xnm2[p % 2]
                xnv = xnvs[p]


                def emit_preamble(p=p, xnv=xnv):
                  for tb in range(8):
                      sl = tb % 2
                      slot = P_lg
                      mats = [(xnm[:, k * PASS_T + tb * 128:
                                   k * PASS_T + (tb + 1) * 128],
                               wgsb[:, k * 8:(k + 1) * 8]) for k in range(CT)]
                      prewar = ((pre_done, pre_done.v - 1),) if (p * 8 + tb) >= 2 \
                          else ()

                      def r_lg(e, sl=sl, s=slot):
                          return e.tensor_tensor(lgt[:, sl * 8:sl * 8 + 8],
                                                 s.ap[:, 0:8], bg_bc[:],
                                                 op=ALU.add)
                      mm_group(sch, slot, slot.ap[:, 0:8], mats,
                               [("vector", r_lg, prewar)],
                               extra_waits=((xnm_q, xnv),))
                      lgv = (slot.free, slot.free.v)

                      def v_m1(e, sl=sl):
                          return e.reduce_max(pst[:, sl * 8:sl * 8 + 1],
                                              lgt[:, sl * 8:sl * 8 + 8], axis=AXX)
                      sch.emit("vector", v_m1, waits=(lgv,), incs=((mch, 1),))
                      mchv = mch.v

                      def v_nm1(e, sl=sl):
                          return e.tensor_scalar(pst[:, sl * 8 + 1:sl * 8 + 2],
                                                 pst[:, sl * 8:sl * 8 + 1],
                                                 -1.0, None, op0=ALU.mult)
                      sch.emit("vector", v_nm1, waits=((mch, mchv),),
                               incs=((moe_vs, 1),))
                      mv = moe_vs.v

                      def s_exp(e, sl=sl):
                          return e.activation(
                              elg[:, sl * 8:sl * 8 + 8],
                              lgt[:, sl * 8:sl * 8 + 8],
                              AF.Exp, bias=pst[:, sl * 8 + 1:sl * 8 + 2],
                              accum_out=pst[:, sl * 8 + 2:sl * 8 + 3])
                      exw = ((moe_vs, mv),) + prewar
                      if p == 0 and tb == 0:
                          exw = exw + ((ld, bgv),)
                      sch.emit("scalar", s_exp, waits=exw,
                               incs=((moe_sv, 1),))
                      sv = moe_sv.v

                      chain = []

                      def estep(fn, waits=(), incs=None):
                          sch.emit("vector", fn,
                                   waits=waits + ((mch, mch.v),),
                                   incs=incs if incs is not None
                                   else ((mch, 1),))
                      estep(lambda e, sl=sl: e.reciprocal(
                          pst[:, sl * 8 + 3:sl * 8 + 4],
                          pst[:, sl * 8 + 2:sl * 8 + 3]), waits=((moe_sv, sv),))
                      estep(lambda e, sl=sl: e.tensor_scalar(
                          elg[:, sl * 8:sl * 8 + 8], elg[:, sl * 8:sl * 8 + 8],
                          pst[:, sl * 8 + 3:sl * 8 + 4], None, op0=ALU.mult))
                      estep(lambda e, sl=sl: e.reduce_max(
                          pst[:, sl * 8 + 4:sl * 8 + 5],
                          elg[:, sl * 8:sl * 8 + 8], axis=AXX))
                      estep(lambda e, sl=sl: e.tensor_scalar(
                          lgt[:, sl * 8:sl * 8 + 8], elg[:, sl * 8:sl * 8 + 8],
                          pst[:, sl * 8 + 4:sl * 8 + 5], None, op0=ALU.is_lt))
                      estep(lambda e, sl=sl: e.tensor_tensor(
                          lgt[:, sl * 8:sl * 8 + 8], lgt[:, sl * 8:sl * 8 + 8],
                          elg[:, sl * 8:sl * 8 + 8], op=ALU.mult))
                      estep(lambda e, sl=sl: e.reduce_max(
                          pst[:, sl * 8 + 5:sl * 8 + 6],
                          lgt[:, sl * 8:sl * 8 + 8], axis=AXX))
                      estep(lambda e, sl=sl: e.tensor_scalar(
                          lgt[:, sl * 8:sl * 8 + 8], elg[:, sl * 8:sl * 8 + 8],
                          pst[:, sl * 8 + 5:sl * 8 + 6], None, op0=ALU.is_ge))
                      estep(lambda e, sl=sl, tb=tb: e.tensor_tensor(
                          w_all[:, tb * 8:(tb + 1) * 8],
                          elg[:, sl * 8:sl * 8 + 8],
                          lgt[:, sl * 8:sl * 8 + 8], op=ALU.mult),
                          incs=((pre_done, 1),))

                  if taps:
                      prew0 = pre_done.v
                      s_dma(tap_aps["tap_w"][:, p * 64:(p + 1) * 64], w_all[:],
                            waits=((pre_done, prew0),))
                for ex in range(NEXP):
                    exg = p * NEXP + ex
                    for half in range(2):
                        hb = half * 2048
                        gq = (p * NEXP + ex) * 2 + half
                        if half == 1:
                            b2war = ((w2_used, 2 * exg - 2),) \
                                if exg >= 2 else ()
                            g_dma(b2s[0:1, (exg % 2) * D:(exg % 2 + 1) * D],
                                  b2r_d[0:1, ex * D:(ex + 1) * D],
                                  waits=b2war)
                        for mc in range(4):
                            gmc = gq * 4 + mc
                            war = ((w1_used, gmc - 1),) if gmc >= 2 else ()
                            w1v = g_dma(
                                w1s[:, (gmc % 2) * CT * 512:
                                    (gmc % 2 + 1) * CT * 512]
                                .rearrange("p (k m) -> p k m", m=512),
                                bass.AP(w1_d.tensor,
                                        ex * D * MH + hb + mc * 512,
                                        [[MH, 128], [128 * MH, CT], [1, 512]]),
                                waits=war)
                            qwar = ((w2_used, gq),) if gq >= 1 else ()
                            w2v = g_dma(
                                w2h[:, mc * 4 * D:(mc + 1) * 4 * D]
                                .rearrange("p (k m) -> p k m", m=D),
                                bass.AP(w2_d.tensor,
                                        ex * MH * D + (hb + mc * 512) * D,
                                        [[D, 128], [128 * D, 4], [1, D]]),
                                waits=qwar)
                            for mtl in range(4):
                                mth = mc * 4 + mtl
                                for nn in range(2):
                                    slot = pnext()
                                    mats = [(w1s[:, (gmc % 2) * CT * 512 +
                                                 k * 512 + mtl * 128:
                                                 (gmc % 2) * CT * 512 +
                                                 k * 512 + (mtl + 1) * 128],
                                             xnm[:, k * PASS_T + nn * 512:
                                                 k * PASS_T + (nn + 1) * 512])
                                            for k in range(CT)]

                                    def r_gel(e, mth=mth, nn=nn, ex=ex,
                                              half=half, s=slot):
                                        return e.activation(
                                            h1h[:, mth * PASS_T + nn * 512:
                                                mth * PASS_T + (nn + 1) * 512],
                                            s.ap[:], AF.Gelu,
                                            bias=b1c[:, ex * 32 + half * 16 + mth:
                                                     ex * 32 + half * 16 + mth + 1])
                                    gel_w = ()
                                    if gq >= 1 and mc == 0 and mtl == 0 and nn == 0:
                                        gel_w = ((w2_used, gq),)
                                    mm_group(
                                        sch, slot, slot.ap[:], mats,
                                        [("scalar", r_gel, gel_w)],
                                        extra_waits=((ld, w1v), (xnm_q, xnv)),
                                        defer_prod=True)
                            sch.tmark(((w1_used, 1),))
                        h1w = tuple((pp.free, pp.free.v) for pp in P)
                        if ex == 0 and half == 0:
                            emit_preamble()

                        if taps and p == 0 and ex == 0 and half == 0:
                            for mt_ in range(16):
                                g_dma(tap_aps["tap_h1"][:, mt_ * PASS_T:
                                                        (mt_ + 1) * PASS_T],
                                      h1h[:, mt_ * PASS_T:(mt_ + 1) * PASS_T],
                                      waits=h1w)
                        for tb in range(8):
                            for dh in range(2):
                                slot = pnext()
                                mats = [(h1h[:, kt * PASS_T + tb * 128:
                                             kt * PASS_T + (tb + 1) * 128],
                                         w2h[:, kt * D + dh * 512:
                                             kt * D + (dh + 1) * 512])
                                        for kt in range(16)]
                                if half == 1:
                                    mats.append(
                                        (ones_bf[:],
                                         b2s[0:1, (exg % 2) * D + dh * 512:
                                             (exg % 2) * D + (dh + 1) * 512]))
                                if half == 0:
                                    def r_y(e, tb=tb, dh=dh, s=slot):
                                        return e.copy(
                                            ystg[:, tb * D + dh * 512:
                                                 tb * D + (dh + 1) * 512],
                                            s.ap[:])
                                    ywait = ((gacc_done, gacc_done.v),) \
                                        if (ex > 0 or p > 0) else ()
                                    mm_group(sch, slot, slot.ap[:], mats,
                                             [("scalar", r_y, ywait)],
                                             extra_waits=((ld, w2v),) + h1w,
                                             defer_prod=True)
                                else:
                                    def r_y2(e, tb=tb, dh=dh, s=slot):
                                        return e.tensor_tensor(
                                            ystg[:, tb * D + dh * 512:
                                                 tb * D + (dh + 1) * 512],
                                            s.ap[:],
                                            ystg[:, tb * D + dh * 512:
                                                 tb * D + (dh + 1) * 512],
                                            op=ALU.add)
                                    mm_group(sch, slot, slot.ap[:], mats,
                                             [("vector", r_y2, ystw)],
                                             extra_waits=((ld, w2v),) + h1w,
                                             defer_prod=True)
                        if half == 0:
                            ystw = tuple((pp.free, pp.free.v) for pp in P)
                        sch.tmark(((w2_used, 1),))
                    if taps and ex == 0:
                        yfw = tuple((pp.free, pp.free.v) for pp in P)
                        s_dma(tap_aps["tap_y"][:, p * 8192:(p + 1) * 8192],
                              ystg[:], waits=yfw)
                    prew = pre_done.v
                    for tb in range(8):
                        for dh in range(2):
                            def v_cmb(e, tb=tb, dh=dh, ex=ex):
                                if ex == 0:
                                    return e.tensor_scalar(
                                        g_acc[:, tb * D + dh * 512:
                                              tb * D + (dh + 1) * 512],
                                        ystg[:, tb * D + dh * 512:
                                             tb * D + (dh + 1) * 512],
                                        w_all[:, tb * 8:tb * 8 + 1],
                                        None, op0=ALU.mult)
                                return e.scalar_tensor_tensor(
                                    g_acc[:, tb * D + dh * 512:
                                          tb * D + (dh + 1) * 512],
                                    ystg[:, tb * D + dh * 512:
                                         tb * D + (dh + 1) * 512],
                                    w_all[:, tb * 8 + ex:tb * 8 + ex + 1],
                                    g_acc[:, tb * D + dh * 512:
                                          tb * D + (dh + 1) * 512],
                                    op0=ALU.mult, op1=ALU.add)
                            sch.emit("vector", v_cmb, waits=((pre_done, prew),),
                                     incs=((gacc_done, 1),))

                def emit_final(p=p, gav=gacc_done.v):
                  for tb in range(8):
                      def s_sil(e, tb=tb):
                          return e.activation(g_acc[:, tb * D:(tb + 1) * D],
                                              g_acc[:, tb * D:(tb + 1) * D],
                                              AF.Silu)
                      sch.emit("scalar", s_sil, waits=((gacc_done, gav),),
                               incs=((gsil, 1),))
                  gsv = gsil.v
                  if taps:
                      for tb in range(8):
                          s_dma(tap_aps["tap_g"][:, p * 8192 + tb * D:
                                                 p * 8192 + (tb + 1) * D],
                                g_acc[:, tb * D:(tb + 1) * D],
                                waits=((gsil, gsv),))
                  for tb in range(8):
                      gtb = p * 8 + tb
                      sl = tb % 2
                      war = ((fin_v, fin_v.v - 1),) if gtb >= 2 else ()
                      owar = ((out_q, 16 * (gtb - 1)),) if gtb >= 2 else ()
                      r0 = gtb * 128
                      s_dma(s_rl[sl][:], s_dram[r0:r0 + 128, :],
                            waits=((s_st, s_st.v),) + war, inc=srl_q)
                      s_dma(x0_t[sl][:],
                            xs_d[128 + WARM + r0:128 + WARM + r0 + 128, :],
                            waits=war, inc=x0_q)
                      sqv2 = srl_q.v
                      xqv2 = x0_q.v

                      def v_sg(e, sl=sl, tb=tb):
                          return e.scalar_tensor_tensor(
                              o_sb[sl][:], s_rl[sl][:], 1.0,
                              g_acc[:, tb * D:(tb + 1) * D],
                              op0=ALU.mult, op1=ALU.mult)
                      sch.emit("vector", v_sg,
                               waits=((srl_q, sqv2), (gsil, gsv)) + owar,
                               incs=((mch, 1),))
                      sgv = mch.v

                      def v_x0(e, sl=sl):
                          return e.tensor_tensor(o_sb[sl][:], o_sb[sl][:],
                                                 x0_t[sl][:], op=ALU.add)
                      sch.emit("vector", v_x0, waits=((x0_q, xqv2), (mch, sgv)),
                               incs=((fin_v, 1),))
                      fv = fin_v.v
                      s_dma(out_d[r0:r0 + 128, :], o_sb[sl][:],
                            waits=((fin_v, fv),), inc=out_q)

                emit_final()

            sch.emit("sync", lambda e: e.nop(), waits=((out_q, out_q.v),))
            sch.flush()

    return nc


_CACHE = {}


def _dwdiag(dw_w):
    dwv = np.ascontiguousarray(
        np.asarray(dw_w, np.float32)[:, 0, :]
        .reshape(8, 128, 4).transpose(1, 0, 2).reshape(128, 32))
    dd = np.zeros((128, 32 * 128), np.float32)
    p = np.arange(128)
    for ck in range(32):
        dd[p, ck * 128 + p] = dwv[:, ck]
    return dd


def _prep(inputs):
    x = np.asarray(inputs["L0_attn_x"], np.float32).reshape(L, D)
    slabs = []
    for i in range(NCORES):
        lo = 2048 * i - WARM - 3
        hi = 2048 * (i + 1)
        if lo < 0:
            slab = np.concatenate(
                [np.zeros((-lo, D), np.float32), x[0:hi]], 0)
        else:
            slab = x[lo:hi]
        slab = np.concatenate([np.zeros((125, D), np.float32), slab], 0)
        assert slab.shape[0] == SLAB
        slabs.append(np.ascontiguousarray(slab))

    def c128(v):
        return np.ascontiguousarray(
            np.asarray(v, np.float32).reshape(8, 128).T)

    wgate = np.zeros((D, FFP), np.float32)
    wgate[:, :2730] = np.asarray(inputs["L0_attn_w_gate"], np.float32)
    wval = np.zeros((D, FFP), np.float32)
    wval[:, :2730] = np.asarray(inputs["L0_attn_w_value"], np.float32)
    wffo = np.zeros((FFP, D), np.float32)
    wffo[:2730, :] = np.asarray(inputs["L0_attn_w_ff_out"], np.float32)
    common = {
        "pwT": np.ascontiguousarray(
            np.asarray(inputs["L0_attn_pw_w"], np.float32).T),
        "whg": np.ascontiguousarray(
            np.asarray(inputs["L0_attn_w_hg"], np.float32).reshape(D, 768)),
        "wout": np.ascontiguousarray(
            np.asarray(inputs["L0_attn_w_out"], np.float32).reshape(1536, 256)),
        "dwdiag": _dwdiag(inputs["L0_attn_dw_w"]),
        "dwb": c128(inputs["L0_attn_dw_b"]),
        "pwb": c128(inputs["L0_attn_pw_b"]),
        "gc": c128(inputs["L0_attn_conv_gamma"]),
        "gg": c128(inputs["L0_attn_gru_gamma"]),
        "gf": c128(inputs["L0_attn_ff_gamma"]),
        "wgate": wgate, "wval": wval, "wffo": wffo,
        "moewg": np.ascontiguousarray(
            np.asarray(inputs["L0_ffn_Wg"], np.float32)
            .reshape(8, 128, 8).transpose(1, 0, 2).reshape(128, 64)),
        "bgx": np.asarray(inputs["L0_ffn_bg"], np.float32).reshape(1, 8),
        "b1c": np.ascontiguousarray(
            np.asarray(inputs["L0_ffn_b1"], np.float32)
            .reshape(8, 32, 128).transpose(2, 0, 1).reshape(128, 256)),
        "b2r": np.ascontiguousarray(
            np.asarray(inputs["L0_ffn_b2"], np.float32).reshape(1, 8192)),
        "w1": np.ascontiguousarray(np.asarray(inputs["L0_ffn_W1"], np.float32)),
        "w2": np.ascontiguousarray(np.asarray(inputs["L0_ffn_W2"], np.float32)),
        "ident": np.eye(128, dtype=np.float32),
    }
    in_maps = []
    for i in range(NCORES):
        m = dict(common)
        m["xs"] = slabs[i]
        m["zsc"] = np.full((128, 1), 0.0 if i == 0 else 1.0, np.float32)
        in_maps.append(m)
    return in_maps


def kernel(**inputs):
    if "nc" not in _CACHE:
        _CACHE["nc"] = build_program(taps=TAPS)
    nc = _CACHE["nc"]
    in_maps = _prep(inputs)
    res = run_bass_kernel_spmd(nc, in_maps, core_ids=list(range(NCORES)))
    out = np.concatenate([np.asarray(r["out"], np.float32)
                          for r in res.results], 0)
    if TAPS:
        kernel.last_results = res
    return out.reshape(1, L, D)
